# version 19
# baseline (speedup 1.0000x reference)
import numpy as np
from contextlib import ExitStack

import concourse.bass as bass
import concourse.mybir as mybir
from concourse.bass_utils import run_bass_kernel_spmd

F32 = mybir.dt.float32
BF16 = mybir.dt.bfloat16
AF = mybir.ActivationFunctionType
ALU = mybir.AluOpType
AX = mybir.AxisListType

H = 2048
FT = 16
B = 4
T = 4096
CORES = 8
TOK = 2048
C = 512
NCH = 4
KH = 8192
KHT = 64
MF = 1024
MFT = 8
E = 8
STRIDE = 516
EPS = 1e-5
NEG = -1e38

ENGINES = ("tensor", "vector", "scalar", "sync", "gpsimd")
PHASE = ["?"]
MM_LOG = []


class Buf:
    __slots__ = ("name", "w", "r")

    def __init__(self, name):
        self.name = name
        self.w = None
        self.r = []


DMA_SEMS = ("cst", "wcst", "lx0", "lx1", "lx2", "lx3", "lx4", "lx5", "lx6", "lx7", "lx8", "lx9", "lx10", "lx11", "lx12", "lx13", "lx14", "lx15", "wd0", "wd1", "wd2", "hd0", "hd1", "hd2", "hd3", "hd4", "hd5", "hd6", "hd7", "wb0", "wb1", "wb2", "xr0", "xr1", "xr2", "xa0", "xa1", "st0", "st1", "gst0", "gst1")


class Sched:
    def __init__(self):
        self.items = {e: [] for e in ENGINES}
        self.count = {s: 0 for s in ENGINES + DMA_SEMS}

    def g(self, eng, emit, reads=(), writes=(), extra=(), dma=False, sem=None):
        sem_name = sem or eng
        deps = []
        for b in reads:
            if b.w is not None:
                deps.append(b.w)
        for b in writes:
            if b.w is not None:
                deps.append(b.w)
            deps.extend(b.r)
        deps.extend(t for t in extra if t is not None)
        waits = list(deps)
        inc = 16 if dma else 1
        self.count[sem_name] += inc
        tok = (sem_name, self.count[sem_name])
        self.items[eng].append((waits, emit, inc, sem_name))
        for b in reads:
            b.r.append(tok)
        for b in writes:
            b.w = tok
            b.r = []
        return tok

    def emit_engine(self, e, eng_name, sems):
        waited = {}
        for waits, emit, inc, sem_name in self.items[eng_name]:
            need = {}
            for (pe, val) in waits:
                if val > need.get(pe, 0):
                    need[pe] = val
            for pe, val in need.items():
                if waited.get(pe, 0) < val:
                    e.wait_ge(sems[pe], val)
                    waited[pe] = val
            emit(e).then_inc(sems[sem_name], inc)


def _host_tile_lhsT(w_io, kt, ot):
    a = w_io.reshape(kt, 128, ot, 128).transpose(2, 1, 0, 3)
    return np.ascontiguousarray(a.reshape(ot, 128, kt * 128))


def _prep_inputs(I):
    d = {}
    d["wk"] = _host_tile_lhsT(np.ascontiguousarray(I["L0_attn_Wk"].T), FT, FT)
    d["wv"] = _host_tile_lhsT(np.ascontiguousarray(I["L0_attn_Wv"].T), FT, FT)
    d["wr"] = _host_tile_lhsT(np.ascontiguousarray(I["L0_attn_Wr"].T), FT, FT)
    d["wo"] = _host_tile_lhsT(np.ascontiguousarray(I["L0_attn_Wo"].T), FT, FT)
    d["wcr"] = _host_tile_lhsT(np.ascontiguousarray(I["L0_attn_Wcr"].T), FT, FT)
    d["wkey"] = _host_tile_lhsT(np.ascontiguousarray(I["L0_attn_Wkey"].T), FT, KHT)
    d["wval"] = _host_tile_lhsT(np.ascontiguousarray(I["L0_attn_Wval"].T), KHT, FT)
    d["gp"] = np.stack([_host_tile_lhsT(np.asarray(I["L0_ffn_gate_proj"][e]), FT, MFT)
                        for e in range(E)])
    d["up"] = np.stack([_host_tile_lhsT(np.asarray(I["L0_ffn_up_proj"][e]), FT, MFT)
                        for e in range(E)])
    d["dp"] = np.stack([_host_tile_lhsT(np.asarray(I["L0_ffn_down_proj"][e]), MFT, FT)
                        for e in range(E)])
    rt = np.asarray(I["L0_ffn_router_w"]).reshape(FT, 128, E).transpose(1, 0, 2)
    d["rtr"] = np.ascontiguousarray(rt.reshape(128, FT * E))
    vecs = [I["L0_attn_ln1_g"], I["L0_attn_ln1_b"], I["L0_attn_ln2_g"],
            I["L0_attn_ln2_b"], I["L0_attn_tm_first"],
            np.asarray(I["L0_attn_tm_mk"]).reshape(H),
            np.asarray(I["L0_attn_tm_mv"]).reshape(H),
            np.asarray(I["L0_attn_tm_mr"]).reshape(H),
            np.asarray(I["L0_attn_cm_mk"]).reshape(H),
            np.asarray(I["L0_attn_cm_mr"]).reshape(H)]
    d["pp"] = np.concatenate(
        [np.ascontiguousarray(np.asarray(v, dtype=np.float32).reshape(FT, 128).T)
         for v in vecs], axis=1)
    d["id8"] = np.eye(8, dtype=np.float32)
    d["id128"] = np.eye(128, dtype=np.float32)
    sel = np.zeros((8, 8 * 128), dtype=np.float32)
    for e in range(E):
        sel[e, e * 128:(e + 1) * 128] = 1.0
    d["sel"] = sel
    for k in d:
        d[k] = np.ascontiguousarray(d[k], dtype=np.float32)

    x = np.asarray(I["L0_attn_x"], dtype=np.float32)
    per_core = []
    for c in range(CORES):
        b, half = c // 2, c % 2
        if half == 0:
            xs = np.zeros((TOK + 2, H), dtype=np.float32)
            xs[2:] = x[b, 0:TOK]
            m = 0.0
        else:
            xs = x[b, TOK - 2:2 * TOK]
            m = 1.0
        cd = dict(d)
        cd["x"] = np.ascontiguousarray(xs.T)
        cd["msk"] = np.full((128, 1), m, dtype=np.float32)
        per_core.append(cd)
    return per_core


P_G1, P_B1, P_G2, P_B2, P_TMF, P_TMK, P_TMV, P_TMR, P_CMK, P_CMR = \
    [i * 16 for i in range(10)]


def build_program():
    nc = bass.Bass("TRN2", target_bir_lowering=False)
    S = Sched()

    dx = nc.dram_tensor("x", [H, TOK + 2], F32, kind="ExternalInput").ap()
    dwk = nc.dram_tensor("wk", [FT, 128, H], F32, kind="ExternalInput").ap()
    dwv = nc.dram_tensor("wv", [FT, 128, H], F32, kind="ExternalInput").ap()
    dwr = nc.dram_tensor("wr", [FT, 128, H], F32, kind="ExternalInput").ap()
    dwo = nc.dram_tensor("wo", [FT, 128, H], F32, kind="ExternalInput").ap()
    dwcr = nc.dram_tensor("wcr", [FT, 128, H], F32, kind="ExternalInput").ap()
    dwkey = nc.dram_tensor("wkey", [KHT, 128, H], F32, kind="ExternalInput").ap()
    dwval = nc.dram_tensor("wval", [FT, 128, KH], F32, kind="ExternalInput").ap()
    dgp = nc.dram_tensor("gp", [E, MFT, 128, H], F32, kind="ExternalInput").ap()
    dup = nc.dram_tensor("up", [E, MFT, 128, H], F32, kind="ExternalInput").ap()
    ddp = nc.dram_tensor("dp", [E, FT, 128, MF], F32, kind="ExternalInput").ap()
    drtr = nc.dram_tensor("rtr", [128, 128], F32, kind="ExternalInput").ap()
    dpp = nc.dram_tensor("pp", [128, 160], F32, kind="ExternalInput").ap()
    did8 = nc.dram_tensor("id8", [8, 8], F32, kind="ExternalInput").ap()
    did128 = nc.dram_tensor("id128", [128, 128], F32, kind="ExternalInput").ap()
    dsel = nc.dram_tensor("sel", [8, 8 * 128], F32, kind="ExternalInput").ap()
    dmsk = nc.dram_tensor("msk", [128, 1], F32, kind="ExternalInput").ap()
    dout = nc.dram_tensor("out", [H, TOK], F32, kind="ExternalOutput").ap()
    NBLK = 192
    dwpk = nc.dram_tensor("wpk", [NBLK, 128, 4096], BF16, kind="Internal").ap()

    es = ExitStack()
    sb = lambda n, sh, dt: es.enter_context(nc.sbuf_tensor(n, sh, dt))
    psd = lambda n, sh: es.enter_context(nc.psum_tensor(n, sh, F32))

    A = sb("A", [128, FT * STRIDE], F32)
    Bf = sb("Bf", [128, FT * STRIDE], F32)
    XNB = sb("XNB", [128, FT * STRIDE], BF16)
    M1 = sb("M1", [128, FT * STRIDE], BF16)
    M2 = sb("M2", [128, FT * STRIDE], BF16)
    M3 = sb("M3", [128, FT * STRIDE], BF16)
    ACT8 = sb("ACT8", [128, MFT * C], BF16)
    CBp = sb("CBp", [128, E * C], BF16)
    PLR = sb("PLR", [128, STRIDE], F32)
    PLM = sb("PLM", [128, STRIDE], F32)
    T0 = sb("T0", [128, STRIDE], F32)
    T1 = sb("T1", [128, STRIDE], F32)
    TM1 = sb("TM1", [128, C], BF16)
    TM2 = sb("TM2", [128, STRIDE], F32)
    ROWS = sb("ROWS", [128, 4 * C], F32)
    TSQ = sb("TSQ", [128, 4 * STRIDE], BF16)
    PO4 = [sb(f"PO{i}", [128, C], F32) for i in range(3)]
    SLOTB = sb("SLOTB", [128, 6 * 2048], BF16)
    H1C = sb("H1C", [128, 2 * FT], F32)
    H2C = sb("H2C", [128, FT], F32)
    RTRB = sb("RTRB", [128, 128], BF16)
    PPB = sb("PPB", [128, 160], F32)
    ID8 = sb("ID8", [8, 8], F32)
    ID128 = sb("ID128", [128, 128], F32)
    SEL = sb("SEL", [8, 8 * 128], BF16)
    MSK = sb("MSK", [128, 1], F32)
    ONEB = sb("ONEB", [128, 1], BF16)
    ONE1 = sb("ONE1", [128, 128], F32)
    SCR = sb("SCR", [8, C], F32)
    CFM = sb("CFM", [8, C], BF16)
    CTM = sb("CTM", [128, 32], F32)
    TT8 = sb("TT8", [128, 48], F32)

    P = [psd(f"p{i}", [128, 520]) for i in range(4)]

    bAt = [Buf(f"A{i}") for i in range(FT)]
    bBt = [Buf(f"B{i}") for i in range(FT)]
    bXt = [Buf(f"X{i}") for i in range(FT)]
    bM1, bM2, bM3 = Buf("M1"), Buf("M2"), Buf("M3")
    bACT, bCB = Buf("ACT"), Buf("CB")
    bPLR, bPLM = Buf("PLR"), Buf("PLM")
    bT0, bT1 = Buf("T0"), Buf("T1")
    bTM1, bTM2 = Buf("TM1"), Buf("TM2")
    bROWS = Buf("ROWS")
    bTSQ = [Buf(f"TSQ{i}") for i in range(4)]
    bPO = [Buf(f"PO{i}") for i in range(3)]
    bSL = [Buf(f"SL{i}") for i in range(6)]
    bH1C, bH2C = Buf("H1C"), Buf("H2C")
    bSCR, bCFM, bCTM, bTT8 = Buf("SCR"), Buf("CFM"), Buf("CTM"), Buf("TT8")
    bP = [Buf(f"p{i}") for i in range(4)]
    bconst = Buf("const")

    def tile(bufap, ft, w, off=0):
        return bufap[:, ft * STRIDE + off: ft * STRIDE + off + w]

    S.g("gpsimd", lambda e: e.dma_start(out=RTRB[:], in_=drtr),
        writes=[bconst], dma=True, sem="wcst")
    S.g("sync", lambda e: e.dma_start(out=PPB[:], in_=dpp), writes=[bconst],
        dma=True, sem="cst")
    S.g("sync", lambda e: e.dma_start(out=ID8[:], in_=did8), writes=[bconst],
        dma=True, sem="cst")
    S.g("sync", lambda e: e.dma_start(out=ID128[:], in_=did128),
        writes=[bconst], dma=True, sem="cst")
    S.g("gpsimd", lambda e: e.dma_start(out=SEL[:], in_=dsel), writes=[bconst],
        dma=True, sem="wcst")
    S.g("sync", lambda e: e.dma_start(out=MSK[:], in_=dmsk), writes=[bconst],
        dma=True, sem="cst")
    S.g("vector", lambda e: e.memset(ONEB[:], 1.0), writes=[bconst])
    S.g("vector", lambda e: e.memset(ONE1[:], 1.0), writes=[bconst])

    slot_i = [0]
    hslot_i = [0]
    blk_map = {}
    conv = set()
    bWPK = [Buf(f"wpk{j}") for j in range(NBLK)]

    def fetch(dram_ap, key):
        a, _, cc = dram_ap.shape
        assert a * cc == 4096
        if key not in blk_map:
            blk_map[key] = len(blk_map)
        blk = blk_map[key]
        if key not in conv:
            conv.add(key)
            p = slot_i[0] % 3
            slot_i[0] += 1
            h0, h1_ = 2 * p, 2 * p + 1
            region = SLOTB[:, h0 * 2048:(h1_ + 1) * 2048]

            def em(e, d=dram_ap, region=region, cc=cc):
                dst = region.rearrange("p (a c) -> p a c", c=cc)
                return e.dma_start(out=dst, in_=d.transpose([1, 0, 2]))
            S.g("gpsimd", em, writes=[bSL[h0], bSL[h1_]], dma=True, sem=f"wd{p}")
            S.g("sync", lambda e, region=region, blk=blk: e.dma_start(
                out=dwpk[blk, :, :], in_=region),
                reads=[bSL[h0], bSL[h1_]], writes=[bWPK[blk]], dma=True,
                sem=f"wb{p}")
            out = []
            for ai in range(a):
                base = h0 * 2048 + ai * cc
                out.append((SLOTB[:, base:base + cc], bSL[h0 + (ai * cc) // 2048]))
            return out
        else:
            out = []
            for half in range(2):
                j = hslot_i[0] % 6
                hslot_i[0] += 1
                S.g("sync", lambda e, j=j, blk=blk, half=half: e.dma_start(
                    out=SLOTB[:, j * 2048:(j + 1) * 2048],
                    in_=dwpk[blk, :, half * 2048:(half + 1) * 2048]),
                    reads=[bWPK[blk]], writes=[bSL[j]], dma=True, sem=f"hd{j}")
                for ai in range(a // 2):
                    out.append((SLOTB[:, j * 2048 + ai * cc:
                                      j * 2048 + (ai + 1) * cc], bSL[j]))
            return out

    pp = lambda base, ft: PPB[:, base + ft: base + ft + 1]

    def acc_group(pk, spans, lhs_of_kt, rhs_of_kt, nkt):
        MM_LOG.append((PHASE[0], len(spans) * nkt))

        def em(e):
            last = None
            for (off, wc) in spans:
                for kt in range(nkt):
                    last = e.matmul(pk[:, off:off + wc], lhs_of_kt(kt),
                                    rhs_of_kt(kt, off, wc),
                                    start=(kt == 0), stop=(kt == nkt - 1))
            return last
        return em

    def emit_chunk(ci):
        cw = 514 if ci == 0 else 512
        mw = 513 if ci == 0 else 512
        xoff = 0 if ci == 0 else 2 + ci * C
        ono = 2 if ci == 0 else 0
        ox1 = 1 if ci == 0 else 0
        spans = [(0, 512), (512, mw - 512)] if mw > 512 else [(0, mw)]
        sspans = [(0, 512), (512, cw - 512)] if cw > 512 else [(0, cw)]
        pcnt = [0]
        resv = [False]

        def nsp():
            while True:
                i = pcnt[0] % 4
                pcnt[0] += 1
                if not (resv[0] and i >= 2):
                    return P[i], bP[i]

        bA_all = list(bAt)
        bB_all = list(bBt)
        bX_all = list(bXt)

        PHASE[0] = "P0load"
        for ft in range(FT):
            S.g("sync", lambda e, ft=ft: e.dma_start(
                out=tile(A[:], ft, cw),
                in_=dx[ft * 128:(ft + 1) * 128, xoff:xoff + cw]),
                writes=[bAt[ft]], dma=True, sem=f"lx{ft}")

        def ln_stats(src_tile, w, off, pS1, bS1, pS2, bS2, src_bufs,
                     src_is_bf):
            wspans = [(0, 512), (512, w - 512)] if w > 512 else [(0, w)]
            for ft in range(FT):
                ra = TSQ[:, (ft % 2) * STRIDE:(ft % 2) * STRIDE + w]
                rsq = TSQ[:, (2 + ft % 2) * STRIDE:(2 + ft % 2) * STRIDE + w]
                bra, brsq = bTSQ[ft % 2], bTSQ[2 + ft % 2]
                if src_is_bf:
                    sq_src = src_tile(ft)
                else:
                    S.g("vector", lambda e, ft=ft, ra=ra: e.tensor_copy(
                        ra, src_tile(ft)), reads=[src_bufs[ft]], writes=[bra])
                    sq_src = ra
                S.g("scalar", lambda e, ft=ft, rsq=rsq, sq_src=sq_src:
                    e.activation(rsq, sq_src, AF.Square),
                    reads=([src_bufs[ft]] if src_is_bf else [bra]),
                    writes=[brsq])
                s1src = src_tile(ft) if src_is_bf else ra
                s1read = src_bufs[ft] if src_is_bf else bra

                MM_LOG.append((PHASE[0], len(wspans)))

                def em1(e, ft=ft, s1src=s1src, wspans=wspans):
                    last = None
                    for (o_, wc) in wspans:
                        last = e.matmul(pS1[0:1, o_:o_ + wc], ONEB[:],
                                        s1src[:, o_:o_ + wc],
                                        start=(ft == 0), stop=(ft == FT - 1))
                    return last
                S.g("tensor", em1, reads=[s1read, bconst], writes=[bS1])

                MM_LOG.append((PHASE[0], len(wspans)))

                def em2(e, ft=ft, rsq=rsq, wspans=wspans):
                    last = None
                    for (o_, wc) in wspans:
                        last = e.matmul(pS2[0:1, o_:o_ + wc], ONEB[:],
                                        rsq[:, o_:o_ + wc],
                                        start=(ft == 0), stop=(ft == FT - 1))
                    return last
                S.g("tensor", em2, reads=[brsq, bconst], writes=[bS2])

        def rows_and_planes(pS1, bS1, pS2, bS2, w, compose_ln1=False,
                            skip_planes=False):
            R0 = T0[0:1, 0:w]
            R1 = T1[0:1, 0:w]
            R2 = TM2[0:1, 0:w]
            S.g("vector", lambda e: e.tensor_scalar_mul(R0, pS1[0:1, 0:w], 1.0 / H),
                reads=[bS1], writes=[bT0], extra=[bT0.w])
            S.g("vector", lambda e: e.tensor_scalar_mul(R1, pS2[0:1, 0:w], 1.0 / H),
                reads=[bS2], writes=[bT1], extra=[bT1.w])
            S.g("vector", lambda e: e.tensor_tensor(R2, R0, R0, ALU.mult),
                reads=[bT0], writes=[bTM2], extra=[bTM2.w])
            S.g("vector", lambda e: e.tensor_tensor(R1, R1, R2, ALU.subtract),
                reads=[bTM2], writes=[bT1], extra=[bT1.w])
            if compose_ln1:
                S.g("vector", lambda e: e.tensor_copy(R2, R1),
                    reads=[bT1], writes=[bTM2], extra=[bTM2.w])
            S.g("vector", lambda e: e.tensor_scalar_add(R1, R1, EPS),
                reads=[], writes=[bT1], extra=[bT1.w])
            S.g("scalar", lambda e: e.activation(R1, R1, AF.Ln),
                reads=[], writes=[bT1], extra=[bT1.w])
            S.g("scalar", lambda e: e.activation(R1, R1, AF.Exp, scale=-0.5),
                reads=[], writes=[bT1], extra=[bT1.w])
            S.g("vector", lambda e: e.tensor_tensor(R0, R0, R1, ALU.mult),
                reads=[bT1], writes=[bT0], extra=[bT0.w])
            rspans = [(0, 512), (512, w - 512)] if w > 512 else [(0, w)]

            def em_rep(e, pq, row, rspans=rspans):
                last = None
                for (off, wc) in rspans:
                    last = e.matmul(pq[:, off:off + wc], ONE1[0:1, :],
                                    row[:, off:off + wc], start=True, stop=True)
                return last
            def repl_planes():
                pr_, bpr_ = nsp()
                MM_LOG.append((PHASE[0], len(rspans)))
                S.g("tensor", lambda e, pr_=pr_: em_rep(e, pr_, R1),
                    reads=[bT1, bconst], writes=[bpr_])
                S.g("vector", lambda e, pr_=pr_: e.tensor_copy(
                    PLR[:, 0:w], pr_[:, 0:w]), reads=[bpr_], writes=[bPLR])
                pm_, bpm_ = nsp()
                MM_LOG.append((PHASE[0], len(rspans)))
                S.g("tensor", lambda e, pm_=pm_: em_rep(e, pm_, R0),
                    reads=[bT0, bconst], writes=[bpm_])
                S.g("vector", lambda e, pm_=pm_: e.tensor_copy(
                    PLM[:, 0:w], pm_[:, 0:w]), reads=[bpm_], writes=[bPLM])
            repl_planes()
            if compose_ln1:
                def fin():
                    S.g("vector", lambda e: e.tensor_tensor(R2, R2, R1, ALU.mult),
                        reads=[bT1], writes=[bTM2], extra=[bTM2.w])
                    S.g("vector", lambda e: e.tensor_tensor(R2, R2, R1, ALU.mult),
                        reads=[bT1], writes=[bTM2], extra=[bTM2.w])
                    S.g("vector", lambda e: e.tensor_scalar_add(R2, R2, EPS),
                        reads=[], writes=[bTM2], extra=[bTM2.w])
                    S.g("scalar", lambda e: e.activation(R2, R2, AF.Ln),
                        reads=[], writes=[bTM2], extra=[bTM2.w])
                    S.g("scalar", lambda e: e.activation(R2, R2, AF.Exp, scale=-0.5),
                        reads=[], writes=[bTM2], extra=[bTM2.w])
                    pr_, bpr_ = nsp()
                    MM_LOG.append((PHASE[0], len(rspans)))
                    S.g("tensor", lambda e, pr_=pr_: em_rep(e, pr_, R2),
                        reads=[bTM2, bconst], writes=[bpr_])
                    S.g("vector", lambda e, pr_=pr_: e.tensor_copy(
                        PLR[:, 0:w], pr_[:, 0:w]), reads=[bpr_], writes=[bPLR])
                return fin
            return None

        def moe_gu(e_, apply_cb=True, defer=False, gate_kt=False):
            def gu_mm(pq, bpq, wa, bwa):
                if gate_kt:
                    for kt in range(FT):
                        MM_LOG.append((PHASE[0], 1))
                        S.g("tensor", lambda e, kt=kt, pq=pq, wa=wa: e.matmul(
                            pq[:, 0:C], wa[:, kt * 128:(kt + 1) * 128],
                            tile(XNB[:], kt, C, ono), start=(kt == 0),
                            stop=(kt == FT - 1)),
                            reads=[bXt[kt], bwa], writes=[bpq])
                else:
                    S.g("tensor", acc_group(
                        pq, [(0, C)],
                        lambda kt, wa=wa: wa[:, kt * 128:(kt + 1) * 128],
                        lambda kt, off_, wc_: tile(XNB[:], kt, C, ono), FT),
                        reads=bX_all + [bwa], writes=[bpq])

            def pair(fp_):
                gv = fetch(dgp[e_, fp_:fp_ + 2], ("gp", e_, fp_))
                uv = fetch(dup[e_, fp_:fp_ + 2], ("up", e_, fp_))
                for oi in range(2):
                    fo = fp_ + oi
                    pg, bpg = nsp()
                    pu, bpu = nsp()
                    wga, bwga = gv[oi]
                    wua, bwua = uv[oi]
                    gu_mm(pg, bpg, wga, bwga)
                    gu_mm(pu, bpu, wua, bwua)
                    S.g("scalar", lambda e, pg=pg: e.activation(
                        TM2[:, 0:C], pg[:, 0:C], AF.Silu),
                        reads=[bpg], writes=[bTM2])
                    if apply_cb:
                        S.g("vector", lambda e, pu=pu: e.tensor_tensor(
                            TM1[:], TM2[:, 0:C], pu[:, 0:C], ALU.mult),
                            reads=[bTM2, bpu], writes=[bTM1])
                        S.g("vector", lambda e, fo=fo, e_=e_: e.tensor_tensor(
                            ACT8[:, fo * C:(fo + 1) * C], TM1[:],
                            CBp[:, e_ * C:(e_ + 1) * C], ALU.mult),
                            reads=[bTM1, bCB], writes=[bACT], extra=[bACT.w])
                    else:
                        S.g("vector", lambda e, fo=fo, pu=pu: e.tensor_tensor(
                            ACT8[:, fo * C:(fo + 1) * C], TM2[:, 0:C],
                            pu[:, 0:C], ALU.mult),
                            reads=[bTM2, bpu], writes=[bACT], extra=[bACT.w])
            if defer:
                return [(lambda fp_=fp_: pair(fp_)) for fp_ in range(0, MFT, 2)]
            for fp_ in range(0, MFT, 2):
                pair(fp_)
            return None

        def moe_down(e_, pre=None, defer=False):
            last_e = (e_ == E - 1)
            shared = {}
            units = []

            def unit(do):
                dp_ = (do // 4) * 4
                j = do % 4
                if j == 0:
                    shared[dp_] = fetch(ddp[e_, dp_:dp_ + 4], ("dp", e_, dp_))
                dv = shared[dp_]
                if pre is not None:
                    pre(do)
                pk, bpk = nsp()
                wda, bwda = dv[j]
                S.g("tensor", acc_group(
                    pk, [(0, C)],
                    lambda fl, wda=wda: wda[:, fl * 128:(fl + 1) * 128],
                    lambda fl, off_, wc_: ACT8[:, fl * C:(fl + 1) * C], MFT),
                    reads=[bACT, bwda], writes=[bpk])
                if not last_e:
                    if e_ == 0:
                        S.g("vector", lambda e, do=do, pk=pk: e.tensor_tensor(
                            tile(Bf[:], do, C), pk[:, 0:C],
                            CBp[:, 0 * C:1 * C], ALU.mult),
                            reads=[bpk, bCB], writes=[bBt[do]])
                    else:
                        S.g("vector", lambda e, do=do, pk=pk: e.tensor_tensor(
                            tile(Bf[:], do, C), tile(Bf[:], do, C),
                            pk[:, 0:C], ALU.add),
                            reads=[bpk], writes=[bBt[do]])
                else:
                    if do % 2 == 0:
                        Ta, bTa, Tb, bTb = T0, bT0, T1, bT1
                        stsem = "st0" if ci == 0 else "gst0"
                    else:
                        Ta, bTa, Tb, bTb = PLR, bPLR, PLM, bPLM
                        stsem = "st1" if ci == 0 else "gst1"
                    own0 = 2 + ci * C

                    def xr_load(k):
                        if k < FT:
                            pi = k % 3
                            S.g("scalar", lambda e, k=k, pi=pi: e.dma_start(
                                out=PO4[pi][:],
                                in_=dx[k * 128:(k + 1) * 128, own0:own0 + C]),
                                writes=[bPO[pi]], dma=True,
                                sem=f"xr{pi}" if pi < 2 else "xr2")
                    if do == 0:
                        for k in range(3):
                            xr_load(k)
                    pi = do % 3
                    S.g("vector", lambda e, do=do, pk=pk, Ta=Ta:
                        e.tensor_tensor(
                            Ta[:, 0:C], tile(Bf[:], do, C), pk[:, 0:C],
                            ALU.add),
                        reads=[bBt[do], bpk], writes=[bTa])
                    S.g("scalar", lambda e, Ta=Ta, Tb=Tb: e.activation(
                        Tb[:, 0:C], Ta[:, 0:C], AF.Silu),
                        reads=[bTa], writes=[bTb])
                    S.g("vector", lambda e, do=do, Ta=Ta, Tb=Tb:
                        e.tensor_tensor(
                            Ta[:, 0:C], Tb[:, 0:C], tile(A[:], do, C, ono),
                            ALU.mult),
                        reads=[bTb, bAt[do]], writes=[bTa], extra=[bTa.w])
                    S.g("vector", lambda e, pi=pi, Ta=Ta: e.tensor_tensor(
                        Ta[:, 0:C], Ta[:, 0:C], PO4[pi][:], ALU.add),
                        reads=[bPO[pi]], writes=[bTa], extra=[bTa.w])
                    xr_load(do + 3)
                    steng = "sync" if ci == 0 else "gpsimd"
                    S.g(steng, lambda e, do=do, Ta=Ta: e.dma_start(
                        out=dout[do * 128:(do + 1) * 128,
                                 ci * C:(ci + 1) * C],
                        in_=Ta[:, 0:C]),
                        reads=[bTa], dma=True, sem=stsem)

            if defer:
                return [(lambda do=do: unit(do)) for do in range(FT)]
            for do in range(FT):
                unit(do)
            return None

        PHASE[0] = "LN0"
        if ci == 0:
            ln_stats(lambda ft: tile(A[:], ft, cw), cw, 0, P[2], bP[2],
                     P[3], bP[3], bA_all, False)
            fin_ln1 = rows_and_planes(P[2], bP[2], P[3], bP[3], cw,
                                      compose_ln1=True)
        else:
            def rep_row(row_ap, base, dst_ap, dst_b):
                pr_, bpr_ = nsp()
                MM_LOG.append((PHASE[0], 1))
                S.g("tensor", lambda e, pr_=pr_, row_ap=row_ap, base=base:
                    e.matmul(pr_[:, 0:cw], ONE1[base:base + 1, :], row_ap,
                             start=True, stop=True),
                    reads=[bROWS, bconst], writes=[bpr_])
                S.g("vector", lambda e, pr_=pr_, dst_ap=dst_ap: e.tensor_copy(
                    dst_ap[:, 0:cw], pr_[:, 0:cw]), reads=[bpr_],
                    writes=[dst_b])
            rep_row(ROWS[0:1, C:C + cw], 0, PLR, bPLR)
            rep_row(ROWS[0:1, 0:cw], 0, PLM, bPLM)

            def fin_ln1():
                rep_row(ROWS[0:1, 2 * C:2 * C + cw], 0, PLR, bPLR)
        xeng = "vector"
        for ft in range(FT):
            S.g(xeng, lambda e, ft=ft: e.tensor_tensor(
                tile(A[:], ft, cw), tile(A[:], ft, cw), PLR[:, 0:cw], ALU.mult),
                reads=[bPLR], writes=[bAt[ft]])
            S.g(xeng, lambda e, ft=ft: e.tensor_tensor(
                tile(A[:], ft, cw), tile(A[:], ft, cw), PLM[:, 0:cw],
                ALU.subtract), reads=[bPLM], writes=[bAt[ft]])
            S.g(xeng, lambda e, ft=ft: e.tensor_copy(
                tile(XNB[:], ft, cw), tile(A[:], ft, cw)),
                reads=[bAt[ft]], writes=[bXt[ft]])
        fin_ln1()

        def h1_sweep_ft(ft, mix_specs):
            hw = cw if ci == 0 else 513
            if ci > 0:
                S.g("vector", lambda e, ft=ft: e.tensor_copy(
                    T0[:, 0:1], H1C[:, ((ci - 1) % 2) * FT + ft:
                                    ((ci - 1) % 2) * FT + ft + 1]),
                    reads=[bH1C], writes=[bT0])
            ho = 0 if ci == 0 else 1
            S.g("vector", lambda e, ft=ft, ho=ho: e.tensor_tensor(
                T0[:, ho:ho + cw], tile(A[:], ft, cw), PLR[:, 0:cw],
                ALU.mult), reads=[bAt[ft], bPLR], writes=[bT0],
                extra=[bT0.w])
            S.g("vector", lambda e, ft=ft, ho=ho: e.tensor_scalar(
                T0[:, ho:ho + cw], T0[:, ho:ho + cw], pp(P_G1, ft),
                pp(P_B1, ft), ALU.mult, ALU.add),
                reads=[bconst], writes=[bT0], extra=[bT0.w])
            if ci == 0:
                S.g("vector", lambda e: e.tensor_scalar_mul(
                    T0[:, 0:2], T0[:, 0:2], MSK[:]),
                    reads=[bconst], writes=[bT0], extra=[bT0.w])
            S.g("vector", lambda e, ft=ft, hw=hw: e.tensor_copy(
                H1C[:, (ci % 2) * FT + ft:(ci % 2) * FT + ft + 1],
                T0[:, hw - 1:hw]),
                reads=[bT0], writes=[bH1C])
            S.g("vector", lambda e, hw=hw: e.tensor_tensor(
                T1[:, 0:hw - 1], T0[:, 1:hw], T0[:, 0:hw - 1],
                ALU.subtract), reads=[bT0], writes=[bT1])
            for (dstap, dstb, pb) in mix_specs:
                S.g("vector", lambda e, ft=ft, dstap=dstap, pb=pb, hw=hw:
                    e.scalar_tensor_tensor(
                        tile(dstap, ft, hw - 1), T1[:, 0:hw - 1],
                        pp(pb, ft), T0[:, 0:hw - 1], ALU.mult, ALU.add),
                    reads=[bT1, bT0, bconst], writes=[dstb],
                    extra=[dstb.w])

        def h1_sweep(mix_specs):
            for ft in range(FT):
                h1_sweep_ft(ft, mix_specs)

        h1_sweep([(M1[:], bM1, P_TMR), (M2[:], bM2, P_TMV)])

        PHASE[0] = "scores"
        p3, bp3 = P[3], bP[3]
        for kt in range(FT):
            MM_LOG.append((PHASE[0], 1))
            S.g("tensor", lambda e, kt=kt: e.matmul(
                p3[0:8, 0:C], RTRB[:, kt * 8:(kt + 1) * 8],
                tile(XNB[:], kt, C, ono), start=(kt == 0),
                stop=(kt == FT - 1)),
                reads=[bXt[kt], bconst], writes=[bp3])
        S.g("vector", lambda e: e.tensor_copy(SCR[:], p3[0:8, 0:C]),
            reads=[bp3], writes=[bSCR])
        PHASE[0] = "e0gu"
        moe_gu(0, apply_cb=False, gate_kt=True)
        for blk in range(4):
            pq, bpq = nsp()
            MM_LOG.append((PHASE[0], 1))
            S.g("tensor", lambda e, blk=blk, pq=pq: e.transpose(
                pq[:, 0:8], SCR[:, blk * 128:(blk + 1) * 128], ID8[:]),
                reads=[bSCR, bconst], writes=[bpq])
            st = TT8[:, 0:8]
            m1 = TT8[:, 8:9]; m2 = TT8[:, 9:10]; nm1 = TT8[:, 10:11]
            wv_ = TT8[:, 16:24]; mk2 = TT8[:, 24:32]; s2 = TT8[:, 32:40]
            S.g("vector", lambda e, pq=pq: e.tensor_copy(st, pq[:, 0:8]),
                reads=[bpq], writes=[bTT8], extra=[bTT8.w])
            S.g("vector", lambda e: e.tensor_reduce(m1, st, AX.X, ALU.max),
                reads=[bTT8], writes=[bTT8], extra=[bTT8.w])
            S.g("vector", lambda e: e.tensor_scalar(s2, st, m1, None, ALU.is_ge),
                reads=[bTT8], writes=[bTT8], extra=[bTT8.w])
            S.g("vector", lambda e: e.scalar_tensor_tensor(
                s2, s2, -3e38, st, ALU.mult, ALU.add),
                reads=[bTT8], writes=[bTT8], extra=[bTT8.w])
            S.g("vector", lambda e: e.tensor_reduce(m2, s2, AX.X, ALU.max),
                reads=[bTT8], writes=[bTT8], extra=[bTT8.w])
            S.g("vector", lambda e: e.tensor_scalar_mul(nm1, m1, -1.0),
                reads=[bTT8], writes=[bTT8], extra=[bTT8.w])
            S.g("scalar", lambda e: e.activation(wv_, st, AF.Exp, bias=nm1),
                reads=[bTT8], writes=[bTT8], extra=[bTT8.w])
            S.g("vector", lambda e: e.tensor_scalar(mk2, st, m2, None, ALU.is_ge),
                reads=[bTT8], writes=[bTT8], extra=[bTT8.w])
            S.g("vector", lambda e: e.tensor_tensor(wv_, wv_, mk2, ALU.mult),
                reads=[bTT8], writes=[bTT8], extra=[bTT8.w])
            S.g("vector", lambda e: e.tensor_tensor(m2, m2, nm1, ALU.add),
                reads=[bTT8], writes=[bTT8], extra=[bTT8.w])
            S.g("scalar", lambda e: e.activation(m2, m2, AF.Exp),
                reads=[bTT8], writes=[bTT8], extra=[bTT8.w])
            S.g("vector", lambda e: e.tensor_scalar_add(m2, m2, 1.0),
                reads=[bTT8], writes=[bTT8], extra=[bTT8.w])
            S.g("vector", lambda e: e.reciprocal(m2, m2),
                reads=[bTT8], writes=[bTT8], extra=[bTT8.w])
            S.g("vector", lambda e, blk=blk: e.tensor_scalar_mul(
                CTM[:, blk * 8:(blk + 1) * 8], wv_, m2),
                reads=[bTT8], writes=[bCTM], extra=[bCTM.w])
        for blk in range(4):
            pq, bpq = nsp()
            MM_LOG.append((PHASE[0], 1))
            S.g("tensor", lambda e, blk=blk, pq=pq: e.transpose(
                pq[0:8, 0:128], CTM[:, blk * 8:(blk + 1) * 8], ID128[:]),
                reads=[bCTM, bconst], writes=[bpq])
            S.g("vector", lambda e, blk=blk, pq=pq: e.tensor_copy(
                CFM[:, blk * 128:(blk + 1) * 128], pq[0:8, 0:128]),
                reads=[bpq], writes=[bCFM], extra=[bCFM.w])
        for e_ in range(E):
            pq, bpq = nsp()
            MM_LOG.append((PHASE[0], 1))
            S.g("tensor", lambda e, e_=e_, pq=pq: e.matmul(
                pq[:, 0:C], SEL[:, e_ * 128:(e_ + 1) * 128], CFM[:],
                start=True, stop=True),
                reads=[bCFM, bconst], writes=[bpq])
            S.g("vector", lambda e, e_=e_, pq=pq: e.tensor_copy(
                CBp[:, e_ * C:(e_ + 1) * C], pq[:, 0:C]),
                reads=[bpq], writes=[bCB], extra=[bCB.w])



        PHASE[0] = "vpass"
        for pr in range(0, FT, 2):
            vv_ = fetch(dwv[pr:pr + 2], ("wv", pr))
            for oi in range(2):
                ot = pr + oi
                pv, bpv = nsp()
                wva, bwva = vv_[oi]
                S.g("tensor", acc_group(
                    pv, spans,
                    lambda kt, wva=wva: wva[:, kt * 128:(kt + 1) * 128],
                    lambda kt, off_, wc_: tile(M2[:], kt, wc_, off_), FT),
                    reads=[bM2, bwva], writes=[bpv])
                S.g("vector", lambda e, ot=ot, pv=pv: e.tensor_copy(
                    tile(M3[:], ot, mw), pv[:, 0:mw]),
                    reads=[bpv], writes=[bM3], extra=[bM3.w])

        moe_down(0)
        PHASE[0] = "e1gu"
        moe_gu(1)

        PHASE[0] = "rpass"
        for pr in range(0, FT, 2):
            rv_ = fetch(dwr[pr:pr + 2], ("wr", pr))
            for oi in range(2):
                ot = pr + oi
                pk, bpk = nsp()
                wra, bwra = rv_[oi]
                S.g("tensor", acc_group(
                    pk, spans,
                    lambda kt, wra=wra: wra[:, kt * 128:(kt + 1) * 128],
                    lambda kt, off_, wc_: tile(M1[:], kt, wc_, off_), FT),
                    reads=[bM1, bwra], writes=[bpk])
                S.g("scalar", lambda e, pk=pk: e.activation(
                    TM2[:, 0:mw], pk[:, 0:mw], AF.Sigmoid),
                    reads=[bpk], writes=[bTM2])
                S.g("vector", lambda e, ot=ot: e.tensor_tensor(
                    tile(M3[:], ot, mw), TM2[:, 0:mw], tile(M3[:], ot, mw),
                    ALU.mult), reads=[bTM2], writes=[bM3], extra=[bM3.w])

        PHASE[0] = "e1dn_e2gu"
        moe_down(1)
        moe_gu(2)

        PHASE[0] = "wo"
        for pr in range(0, FT, 2):
            ov_ = fetch(dwo[pr:pr + 2], ("wo", pr))
            for oi in range(2):
                ot = pr + oi
                pk, bpk = nsp()
                woa, bwoa = ov_[oi]
                S.g("tensor", acc_group(
                    pk, spans,
                    lambda kt, woa=woa: woa[:, kt * 128:(kt + 1) * 128],
                    lambda kt, off_, wc_: tile(M3[:], kt, wc_, off_), FT),
                    reads=[bM3, bwoa], writes=[bpk])
                S.g("vector", lambda e, ot=ot, pk=pk: e.tensor_tensor(
                    tile(A[:], ot, mw, ox1), tile(A[:], ot, mw, ox1),
                    pk[:, 0:mw], ALU.add),
                    reads=[bpk], writes=[bAt[ot]])

        PHASE[0] = "LN2"
        ln_stats(lambda ft: tile(A[:], ft, mw, ox1), mw, ox1, P[2], bP[2],
                 P[3], bP[3], bA_all, False)
        rows_and_planes(P[2], bP[2], P[3], bP[3], mw)

        def h2_ft(ft):
            if ci > 0:
                S.g("vector", lambda e, ft=ft: e.tensor_copy(
                    T0[:, 0:1], H2C[:, ft:ft + 1]),
                    reads=[bH2C], writes=[bT0])
            ho = 0 if ci == 0 else 1
            S.g("vector", lambda e, ft=ft, ho=ho: e.tensor_tensor(
                T0[:, ho:ho + mw], tile(A[:], ft, mw, ox1), PLR[:, 0:mw],
                ALU.mult), reads=[bAt[ft], bPLR], writes=[bT0], extra=[bT0.w])
            S.g("vector", lambda e, ho=ho: e.tensor_tensor(
                T0[:, ho:ho + mw], T0[:, ho:ho + mw], PLM[:, 0:mw],
                ALU.subtract), reads=[bPLM], writes=[bT0], extra=[bT0.w])
            S.g("vector", lambda e, ft=ft, ho=ho: e.tensor_scalar(
                T0[:, ho:ho + mw], T0[:, ho:ho + mw], pp(P_G2, ft),
                pp(P_B2, ft), ALU.mult, ALU.add),
                reads=[bconst], writes=[bT0], extra=[bT0.w])
            if ci == 0:
                S.g("vector", lambda e: e.tensor_scalar_mul(
                    T0[:, 0:1], T0[:, 0:1], MSK[:]),
                    reads=[bconst], writes=[bT0], extra=[bT0.w])
            S.g("vector", lambda e, ft=ft: e.tensor_copy(
                H2C[:, ft:ft + 1], T0[:, 512:513]),
                reads=[bT0], writes=[bH2C])
            S.g("vector", lambda e: e.tensor_tensor(
                T1[:, 0:C], T0[:, 1:513], T0[:, 0:C], ALU.subtract),
                reads=[bT0], writes=[bT1])
            S.g("vector", lambda e, ft=ft: e.scalar_tensor_tensor(
                tile(M1[:], ft, C), T1[:, 0:C], pp(P_CMK, ft),
                T0[:, 0:C], ALU.mult, ALU.add),
                reads=[bT1, bT0, bconst], writes=[bM1], extra=[bM1.w])
            S.g("vector", lambda e, ft=ft: e.scalar_tensor_tensor(
                tile(M3[:], ft, C), T1[:, 0:C], pp(P_CMR, ft),
                T0[:, 0:C], ALU.mult, ALU.add),
                reads=[bT1, bT0, bconst], writes=[bM3], extra=[bM3.w])

        PHASE[0] = "e2dn_h2"
        moe_down(2, pre=h2_ft)
        PHASE[0] = "e3gu"
        moe_gu(3)

        PHASE[0] = "wcr"
        for pr in range(0, FT, 2):
            cv_ = fetch(dwcr[pr:pr + 2], ("wcr", pr))
            for oi in range(2):
                ot = pr + oi
                pk, bpk = nsp()
                wca, bwca = cv_[oi]
                S.g("tensor", acc_group(
                    pk, [(0, C)],
                    lambda kt, wca=wca: wca[:, kt * 128:(kt + 1) * 128],
                    lambda kt, off_, wc_: tile(M3[:], kt, C), FT),
                    reads=[bM3, bwca], writes=[bpk])
                S.g("scalar", lambda e, pk=pk: e.activation(
                    TM2[:, 0:C], pk[:, 0:C], AF.Sigmoid),
                    reads=[bpk], writes=[bTM2])
                S.g("vector", lambda e, ot=ot: e.tensor_copy(
                    tile(M2[:], ot, C), TM2[:, 0:C]),
                    reads=[bTM2], writes=[bM2], extra=[bM2.w])

        PHASE[0] = "e3dn"
        moe_down(3)

        es_units = None
        if ci < NCH - 1:
            xoff1 = 2 + (ci + 1) * C

            def es_unit(u):
                for ft in (2 * u, 2 * u + 1):
                    sl = ft % 2
                    ra = TSQ[:, sl * STRIDE:sl * STRIDE + C]
                    rsq = TSQ[:, (2 + sl) * STRIDE:(2 + sl) * STRIDE + C]
                    bra, brsq = bTSQ[sl], bTSQ[2 + sl]
                    S.g("gpsimd", lambda e, ft=ft, ra=ra: e.dma_start(
                        out=ra, in_=dx[ft * 128:(ft + 1) * 128,
                                       xoff1:xoff1 + C]),
                        writes=[bra], dma=True, sem=f"xa{sl}")
                    S.g("scalar", lambda e, ra=ra, rsq=rsq: e.activation(
                        rsq, ra, AF.Square), reads=[bra], writes=[brsq])
                    MM_LOG.append((PHASE[0], 1))
                    S.g("tensor", lambda e, ft=ft, ra=ra: e.matmul(
                        P[2][0:1, 0:C], ONEB[:], ra, start=(ft == 0),
                        stop=(ft == FT - 1)),
                        reads=[bra, bconst], writes=[bP[2]])
                    MM_LOG.append((PHASE[0], 1))
                    S.g("tensor", lambda e, ft=ft, rsq=rsq: e.matmul(
                        P[3][0:1, 0:C], ONEB[:], rsq, start=(ft == 0),
                        stop=(ft == FT - 1)),
                        reads=[brsq, bconst], writes=[bP[3]])
            es_units = [lambda u=u: es_unit(u) for u in range(8)]

        PHASE[0] = "wgrp"
        for grp in range(4):
            PHASE[0] = f"e{4+grp}gu"
            gu_units = moe_gu(4 + grp, defer=True)
            PHASE[0] = f"wkey{grp}"
            for fp_ in range(0, 16, 2):
                fo0 = grp * 16 + fp_
                if fp_ % 4 == 0:
                    gu_units[fp_ // 4]()
                kyv = fetch(dwkey[fo0:fo0 + 2], ("wkey", fo0))
                for oi in range(2):
                    fl = fp_ + oi
                    pk, bpk = nsp()
                    wya, bwya = kyv[oi]
                    S.g("tensor", acc_group(
                        pk, [(0, C)],
                        lambda kt, wya=wya: wya[:, kt * 128:(kt + 1) * 128],
                        lambda kt, off_, wc_: tile(M1[:], kt, C), FT),
                        reads=[bM1, bwya], writes=[bpk])
                    S.g("scalar", lambda e, fl=fl, pk=pk: e.activation(
                        tile(M3[:], fl, C), pk[:, 0:C], AF.Silu),
                        reads=[bpk], writes=[bM3], extra=[bM3.w])
            PHASE[0] = f"wval{grp}"
            if grp < 3:
                PHASE_units = None
                units = moe_down(4 + grp, defer=True)
            else:
                PHASE[0] = "finale"
                units = moe_down(7, defer=True)
                PHASE[0] = f"wval{grp}"
                if es_units is not None:
                    resv[0] = True
            for op_ in range(0, 16, 2):
                if grp == 3 and es_units is not None:
                    es_units[op_ // 2]()
                wvv = fetch(dwval[op_:op_ + 2, :, grp * 2048:(grp + 1) * 2048], ("wval", grp, op_))
                for oi in range(2):
                    ov = op_ + oi
                    pk, bpk = nsp()
                    wvla, bwvla = wvv[oi]
                    S.g("tensor", acc_group(
                        pk, [(0, C)],
                        lambda fl, wvla=wvla: wvla[:, fl * 128:(fl + 1) * 128],
                        lambda fl, off_, wc_: tile(M3[:], fl, C), FT),
                        reads=[bM3, bwvla], writes=[bpk])
                    S.g("vector", lambda e, ov=ov, pk=pk: e.tensor_tensor(
                        TM2[:, 0:C], pk[:, 0:C], tile(M2[:], ov, C), ALU.mult),
                        reads=[bpk, bM2], writes=[bTM2])
                    S.g("vector", lambda e, ov=ov: e.tensor_tensor(
                        tile(A[:], ov, C, ono), tile(A[:], ov, C, ono),
                        TM2[:, 0:C], ALU.add),
                        reads=[bTM2], writes=[bAt[ov]])
                    units[ov]()

        if es_units is not None:
            R0e = ROWS[0:1, 0:C]
            R1e = ROWS[0:1, C:2 * C]
            R2e = ROWS[0:1, 2 * C:3 * C]
            R3e = ROWS[0:1, 3 * C:4 * C]

            def rv(fn, rd=()):
                S.g("vector", fn,
                    reads=[bP[2], bP[3]] if rd == "p" else [],
                    writes=[bROWS], extra=[bROWS.w])

            def rs(fn):
                S.g("scalar", fn, reads=[], writes=[bROWS], extra=[bROWS.w])
            rv(lambda e: e.tensor_scalar_mul(R0e, P[2][0:1, 0:C], 1.0 / H),
               rd="p")
            rv(lambda e: e.tensor_scalar_mul(R1e, P[3][0:1, 0:C], 1.0 / H),
               rd="p")
            rv(lambda e: e.tensor_tensor(R3e, R0e, R0e, ALU.mult))
            rv(lambda e: e.tensor_tensor(R1e, R1e, R3e, ALU.subtract))
            rv(lambda e: e.tensor_copy(R2e, R1e))
            rv(lambda e: e.tensor_scalar_add(R1e, R1e, EPS))
            rs(lambda e: e.activation(R1e, R1e, AF.Ln))
            rs(lambda e: e.activation(R1e, R1e, AF.Exp, scale=-0.5))
            rv(lambda e: e.tensor_tensor(R0e, R0e, R1e, ALU.mult))
            rv(lambda e: e.tensor_tensor(R2e, R2e, R1e, ALU.mult))
            rv(lambda e: e.tensor_tensor(R2e, R2e, R1e, ALU.mult))
            rv(lambda e: e.tensor_scalar_add(R2e, R2e, EPS))
            rs(lambda e: e.activation(R2e, R2e, AF.Ln))
            rs(lambda e: e.activation(R2e, R2e, AF.Exp, scale=-0.5))
            resv[0] = False


    for ci in range(NCH):
        PHASE[0] = f"c{ci}"
        emit_chunk(ci)
    assert len(blk_map) == NBLK, f"block count {len(blk_map)} != NBLK {NBLK}"

    nst = {k: S.count[k] for k in ("st0", "st1", "gst0", "gst1")}

    sem_es = ExitStack()
    sems = {}
    for name in ENGINES + DMA_SEMS:
        sems[name] = sem_es.enter_context(nc.semaphore(f"s_{name}"))
    with nc.Block() as block:

        @block.tensor
        def _(e):
            S.emit_engine(e, "tensor", sems)

        @block.vector
        def _(e):
            S.emit_engine(e, "vector", sems)

        @block.scalar
        def _(e):
            S.emit_engine(e, "scalar", sems)

        @block.sync
        def _(e):
            S.emit_engine(e, "sync", sems)
            for k_ in ("st0", "st1", "gst0", "gst1"):
                e.wait_ge(sems[k_], nst[k_])

        @block.gpsimd
        def _(e):
            S.emit_engine(e, "gpsimd", sems)

    sem_es.close()
    es.close()
    return nc


_NC_CACHE = [None]


def kernel(**inputs):
    per_core = _prep_inputs(inputs)
    if _NC_CACHE[0] is None:
        _NC_CACHE[0] = build_program()
    nc = _NC_CACHE[0]
    res = run_bass_kernel_spmd(nc, per_core, core_ids=list(range(CORES)))
    out = np.empty((B, T, H), dtype=np.float32)
    for c in range(CORES):
        b, half = c // 2, c % 2
        out[b, half * TOK:(half + 1) * TOK, :] = res.results[c]["out"].T
    return out


# version 20
# speedup vs baseline: 1.1072x; 1.1072x over previous
import numpy as np
from contextlib import ExitStack

import concourse.bass as bass
import concourse.mybir as mybir
from concourse.bass_utils import run_bass_kernel_spmd

F32 = mybir.dt.float32
BF16 = mybir.dt.bfloat16
AF = mybir.ActivationFunctionType
ALU = mybir.AluOpType
AX = mybir.AxisListType

H = 2048
FT = 16
B = 4
T = 4096
CORES = 8
TOK = 2048
C = 512
NCH = 4
KH = 8192
KHT = 64
MF = 1024
MFT = 8
E = 8
STRIDE = 516
EPS = 1e-5
NEG = -1e38

ENGINES = ("tensor", "vector", "scalar", "sync", "gpsimd")
PHASE = ["?"]
MM_LOG = []


class Buf:
    __slots__ = ("name", "w", "r")

    def __init__(self, name):
        self.name = name
        self.w = None
        self.r = []


DMA_SEMS = ("cst", "wcst", "lx0", "lx1", "lx2", "lx3", "lx4", "lx5", "lx6", "lx7", "lx8", "lx9", "lx10", "lx11", "lx12", "lx13", "lx14", "lx15", "wd0", "wd1", "wd2", "hd0", "hd1", "hd2", "hd3", "hd4", "hd5", "hd6", "hd7", "wb0", "wb1", "wb2", "xr0", "xr1", "xr2", "xa0", "xa1", "st0", "st1", "gst0", "gst1")


class Sched:
    def __init__(self):
        self.items = {e: [] for e in ENGINES}
        self.count = {s: 0 for s in ENGINES + DMA_SEMS}

    def g(self, eng, emit, reads=(), writes=(), extra=(), dma=False, sem=None):
        sem_name = sem or eng
        deps = []
        for b in reads:
            if b.w is not None:
                deps.append(b.w)
        for b in writes:
            if b.w is not None:
                deps.append(b.w)
            deps.extend(b.r)
        deps.extend(t for t in extra if t is not None)
        waits = list(deps)
        inc = 16 if dma else 1
        self.count[sem_name] += inc
        tok = (sem_name, self.count[sem_name])
        self.items[eng].append((waits, emit, inc, sem_name))
        for b in reads:
            b.r.append(tok)
        for b in writes:
            b.w = tok
            b.r = []
        return tok

    def emit_engine(self, e, eng_name, sems):
        waited = {}
        for waits, emit, inc, sem_name in self.items[eng_name]:
            need = {}
            for (pe, val) in waits:
                if val > need.get(pe, 0):
                    need[pe] = val
            for pe, val in need.items():
                if waited.get(pe, 0) < val:
                    e.wait_ge(sems[pe], val)
                    waited[pe] = val
            emit(e).then_inc(sems[sem_name], inc)


def _host_tile_lhsT(w_io, kt, ot):
    a = w_io.reshape(kt, 128, ot, 128).transpose(2, 1, 0, 3)
    return np.ascontiguousarray(a.reshape(ot, 128, kt * 128))


def _prep_inputs(I):
    d = {}
    d["wk"] = _host_tile_lhsT(np.ascontiguousarray(I["L0_attn_Wk"].T), FT, FT)
    d["wv"] = _host_tile_lhsT(np.ascontiguousarray(I["L0_attn_Wv"].T), FT, FT)
    d["wr"] = _host_tile_lhsT(np.ascontiguousarray(I["L0_attn_Wr"].T), FT, FT)
    d["wo"] = _host_tile_lhsT(np.ascontiguousarray(I["L0_attn_Wo"].T), FT, FT)
    d["wcr"] = _host_tile_lhsT(np.ascontiguousarray(I["L0_attn_Wcr"].T), FT, FT)
    d["wkey"] = _host_tile_lhsT(np.ascontiguousarray(I["L0_attn_Wkey"].T), FT, KHT)
    d["wval"] = _host_tile_lhsT(np.ascontiguousarray(I["L0_attn_Wval"].T), KHT, FT)
    d["gp"] = np.stack([_host_tile_lhsT(np.asarray(I["L0_ffn_gate_proj"][e]), FT, MFT)
                        for e in range(E)])
    d["up"] = np.stack([_host_tile_lhsT(np.asarray(I["L0_ffn_up_proj"][e]), FT, MFT)
                        for e in range(E)])
    d["dp"] = np.stack([_host_tile_lhsT(np.asarray(I["L0_ffn_down_proj"][e]), MFT, FT)
                        for e in range(E)])
    rt = np.asarray(I["L0_ffn_router_w"]).reshape(FT, 128, E).transpose(1, 0, 2)
    d["rtr"] = np.ascontiguousarray(rt.reshape(128, FT * E))
    vecs = [I["L0_attn_ln1_g"], I["L0_attn_ln1_b"], I["L0_attn_ln2_g"],
            I["L0_attn_ln2_b"], I["L0_attn_tm_first"],
            np.asarray(I["L0_attn_tm_mk"]).reshape(H),
            np.asarray(I["L0_attn_tm_mv"]).reshape(H),
            np.asarray(I["L0_attn_tm_mr"]).reshape(H),
            np.asarray(I["L0_attn_cm_mk"]).reshape(H),
            np.asarray(I["L0_attn_cm_mr"]).reshape(H)]
    d["pp"] = np.concatenate(
        [np.ascontiguousarray(np.asarray(v, dtype=np.float32).reshape(FT, 128).T)
         for v in vecs], axis=1)
    d["id8"] = np.eye(8, dtype=np.float32)
    d["id128"] = np.eye(128, dtype=np.float32)
    sel = np.zeros((8, 8 * 128), dtype=np.float32)
    for e in range(E):
        sel[e, e * 128:(e + 1) * 128] = 1.0
    d["sel"] = sel
    for k in d:
        d[k] = np.ascontiguousarray(d[k], dtype=np.float32)

    x = np.asarray(I["L0_attn_x"], dtype=np.float32)
    per_core = []
    for c in range(CORES):
        b, half = c // 2, c % 2
        if half == 0:
            xs = np.zeros((TOK + 2, H), dtype=np.float32)
            xs[2:] = x[b, 0:TOK]
            m = 0.0
        else:
            xs = x[b, TOK - 2:2 * TOK]
            m = 1.0
        cd = dict(d)
        cd["x"] = np.ascontiguousarray(xs.T)
        cd["msk"] = np.full((128, 1), m, dtype=np.float32)
        per_core.append(cd)
    return per_core


P_G1, P_B1, P_G2, P_B2, P_TMF, P_TMK, P_TMV, P_TMR, P_CMK, P_CMR = \
    [i * 16 for i in range(10)]


def build_program():
    nc = bass.Bass("TRN2", target_bir_lowering=False)
    S = Sched()

    dx = nc.dram_tensor("x", [H, TOK + 2], F32, kind="ExternalInput").ap()
    dwk = nc.dram_tensor("wk", [FT, 128, H], F32, kind="ExternalInput").ap()
    dwv = nc.dram_tensor("wv", [FT, 128, H], F32, kind="ExternalInput").ap()
    dwr = nc.dram_tensor("wr", [FT, 128, H], F32, kind="ExternalInput").ap()
    dwo = nc.dram_tensor("wo", [FT, 128, H], F32, kind="ExternalInput").ap()
    dwcr = nc.dram_tensor("wcr", [FT, 128, H], F32, kind="ExternalInput").ap()
    dwkey = nc.dram_tensor("wkey", [KHT, 128, H], F32, kind="ExternalInput").ap()
    dwval = nc.dram_tensor("wval", [FT, 128, KH], F32, kind="ExternalInput").ap()
    dgp = nc.dram_tensor("gp", [E, MFT, 128, H], F32, kind="ExternalInput").ap()
    dup = nc.dram_tensor("up", [E, MFT, 128, H], F32, kind="ExternalInput").ap()
    ddp = nc.dram_tensor("dp", [E, FT, 128, MF], F32, kind="ExternalInput").ap()
    drtr = nc.dram_tensor("rtr", [128, 128], F32, kind="ExternalInput").ap()
    dpp = nc.dram_tensor("pp", [128, 160], F32, kind="ExternalInput").ap()
    did8 = nc.dram_tensor("id8", [8, 8], F32, kind="ExternalInput").ap()
    did128 = nc.dram_tensor("id128", [128, 128], F32, kind="ExternalInput").ap()
    dsel = nc.dram_tensor("sel", [8, 8 * 128], F32, kind="ExternalInput").ap()
    dmsk = nc.dram_tensor("msk", [128, 1], F32, kind="ExternalInput").ap()
    dout = nc.dram_tensor("out", [H, TOK], F32, kind="ExternalOutput").ap()
    NBLK = 192
    dwpk = nc.dram_tensor("wpk", [NBLK, 128, 4096], BF16, kind="Internal").ap()

    es = ExitStack()
    sb = lambda n, sh, dt: es.enter_context(nc.sbuf_tensor(n, sh, dt))
    psd = lambda n, sh: es.enter_context(nc.psum_tensor(n, sh, F32))

    A = sb("A", [128, FT * STRIDE], F32)
    Bf = sb("Bf", [128, FT * STRIDE], F32)
    XNB = sb("XNB", [128, FT * STRIDE], BF16)
    M1 = sb("M1", [128, FT * STRIDE], BF16)
    M2 = sb("M2", [128, FT * STRIDE], BF16)
    M3 = sb("M3", [128, FT * STRIDE], BF16)
    ACT8 = sb("ACT8", [128, MFT * C], BF16)
    CBp = sb("CBp", [128, E * C], BF16)
    PLR = sb("PLR", [128, STRIDE], F32)
    PLM = sb("PLM", [128, STRIDE], F32)
    T0 = sb("T0", [128, STRIDE], F32)
    T1 = sb("T1", [128, STRIDE], F32)
    TM1 = sb("TM1", [128, C], BF16)
    TM2 = sb("TM2", [128, STRIDE], F32)
    ROWS = sb("ROWS", [128, 3 * C], F32)
    TSQ = sb("TSQ", [128, 4 * STRIDE], BF16)
    PO4 = [sb(f"PO{i}", [128, C], F32) for i in range(3)]
    SLOTB = sb("SLOTB", [128, 7 * 2048], BF16)
    H1C = sb("H1C", [128, 2 * FT], F32)
    H2C = sb("H2C", [128, FT], F32)
    RTRB = sb("RTRB", [128, 128], BF16)
    PPB = sb("PPB", [128, 160], F32)
    ID8 = sb("ID8", [8, 8], F32)
    ID128 = sb("ID128", [128, 128], F32)
    SEL = sb("SEL", [8, 8 * 128], BF16)
    MSK = sb("MSK", [128, 1], F32)
    ONEB = sb("ONEB", [128, 1], BF16)
    ONE1 = sb("ONE1", [128, 128], F32)
    SCR = sb("SCR", [8, C], F32)
    CFM = sb("CFM", [8, C], BF16)
    CTM = sb("CTM", [128, 32], F32)
    TT8 = sb("TT8", [128, 48], F32)

    P = [psd(f"p{i}", [128, 520]) for i in range(4)]

    bAt = [Buf(f"A{i}") for i in range(FT)]
    bBt = [Buf(f"B{i}") for i in range(FT)]
    bXt = [Buf(f"X{i}") for i in range(FT)]
    bM1, bM2, bM3 = Buf("M1"), Buf("M2"), Buf("M3")
    bACT, bCB = Buf("ACT"), Buf("CB")
    bPLR, bPLM = Buf("PLR"), Buf("PLM")
    bT0, bT1 = Buf("T0"), Buf("T1")
    bTM1, bTM2 = Buf("TM1"), Buf("TM2")
    bROWS = Buf("ROWS")
    bTSQ = [Buf(f"TSQ{i}") for i in range(4)]
    bPO = [Buf(f"PO{i}") for i in range(3)]
    bSL = [Buf(f"SL{i}") for i in range(7)]
    bH1C, bH2C = Buf("H1C"), Buf("H2C")
    bSCR, bCFM, bCTM, bTT8 = Buf("SCR"), Buf("CFM"), Buf("CTM"), Buf("TT8")
    bP = [Buf(f"p{i}") for i in range(4)]
    bconst = Buf("const")

    def tile(bufap, ft, w, off=0):
        return bufap[:, ft * STRIDE + off: ft * STRIDE + off + w]

    S.g("gpsimd", lambda e: e.dma_start(out=RTRB[:], in_=drtr),
        writes=[bconst], dma=True, sem="wcst")
    S.g("sync", lambda e: e.dma_start(out=PPB[:], in_=dpp), writes=[bconst],
        dma=True, sem="cst")
    S.g("sync", lambda e: e.dma_start(out=ID8[:], in_=did8), writes=[bconst],
        dma=True, sem="cst")
    S.g("sync", lambda e: e.dma_start(out=ID128[:], in_=did128),
        writes=[bconst], dma=True, sem="cst")
    S.g("gpsimd", lambda e: e.dma_start(out=SEL[:], in_=dsel), writes=[bconst],
        dma=True, sem="wcst")
    S.g("sync", lambda e: e.dma_start(out=MSK[:], in_=dmsk), writes=[bconst],
        dma=True, sem="cst")
    S.g("vector", lambda e: e.memset(ONEB[:], 1.0), writes=[bconst])
    S.g("vector", lambda e: e.memset(ONE1[:], 1.0), writes=[bconst])

    slot_i = [0]
    hslot_i = [0]
    blk_map = {}
    conv = set()
    bWPK = [Buf(f"wpk{j}") for j in range(NBLK)]

    def fetch(dram_ap, key):
        a, _, cc = dram_ap.shape
        assert a * cc == 4096
        if key not in blk_map:
            blk_map[key] = len(blk_map)
        blk = blk_map[key]
        if key not in conv:
            conv.add(key)
            p = slot_i[0] % 3
            slot_i[0] += 1
            h0, h1_ = 2 * p, 2 * p + 1
            region = SLOTB[:, h0 * 2048:(h1_ + 1) * 2048]

            def em(e, d=dram_ap, region=region, cc=cc):
                dst = region.rearrange("p (a c) -> p a c", c=cc)
                return e.dma_start(out=dst, in_=d.transpose([1, 0, 2]))
            S.g("gpsimd", em, writes=[bSL[h0], bSL[h1_]], dma=True, sem=f"wd{p}")
            S.g("sync", lambda e, region=region, blk=blk: e.dma_start(
                out=dwpk[blk, :, :], in_=region),
                reads=[bSL[h0], bSL[h1_]], writes=[bWPK[blk]], dma=True,
                sem=f"wb{p}")
            out = []
            for ai in range(a):
                base = h0 * 2048 + ai * cc
                out.append((SLOTB[:, base:base + cc], bSL[h0 + (ai * cc) // 2048]))
            return out
        else:
            out = []
            for half in range(2):
                j = hslot_i[0] % 7
                hslot_i[0] += 1
                S.g("sync", lambda e, j=j, blk=blk, half=half: e.dma_start(
                    out=SLOTB[:, j * 2048:(j + 1) * 2048],
                    in_=dwpk[blk, :, half * 2048:(half + 1) * 2048]),
                    reads=[bWPK[blk]], writes=[bSL[j]], dma=True, sem=f"hd{j}")
                for ai in range(a // 2):
                    out.append((SLOTB[:, j * 2048 + ai * cc:
                                      j * 2048 + (ai + 1) * cc], bSL[j]))
            return out

    pp = lambda base, ft: PPB[:, base + ft: base + ft + 1]

    def acc_group(pk, spans, lhs_of_kt, rhs_of_kt, nkt):
        MM_LOG.append((PHASE[0], len(spans) * nkt))

        def em(e):
            last = None
            for (off, wc) in spans:
                for kt in range(nkt):
                    last = e.matmul(pk[:, off:off + wc], lhs_of_kt(kt),
                                    rhs_of_kt(kt, off, wc),
                                    start=(kt == 0), stop=(kt == nkt - 1))
            return last
        return em

    def emit_chunk(ci):
        cw = 514 if ci == 0 else 512
        mw = 513 if ci == 0 else 512
        xoff = 0 if ci == 0 else 2 + ci * C
        ono = 2 if ci == 0 else 0
        ox1 = 1 if ci == 0 else 0
        spans = [(0, 512), (512, mw - 512)] if mw > 512 else [(0, mw)]
        sspans = [(0, 512), (512, cw - 512)] if cw > 512 else [(0, cw)]
        pcnt = [0]
        resv = [False]

        def nsp():
            while True:
                i = pcnt[0] % 4
                pcnt[0] += 1
                if not (resv[0] and i >= 2):
                    return P[i], bP[i]

        bA_all = list(bAt)
        bB_all = list(bBt)
        bX_all = list(bXt)

        PHASE[0] = "P0load"
        if ci == 0:
            for ft in range(FT):
                S.g("gpsimd", lambda e, ft=ft: e.dma_start(
                    out=tile(A[:], ft, cw),
                    in_=dx[ft * 128:(ft + 1) * 128, xoff:xoff + cw]),
                    writes=[bAt[ft]], dma=True, sem=f"lx{ft}")

        def rep_row(row_ap, dst_ap, dst_b, w):
            pr_, bpr_ = nsp()
            MM_LOG.append((PHASE[0], 1))
            S.g("tensor", lambda e, pr_=pr_, row_ap=row_ap: e.matmul(
                pr_[:, 0:w], ONE1[0:1, :], row_ap, start=True, stop=True),
                reads=[bROWS, bconst], writes=[bpr_])
            S.g("vector", lambda e, pr_=pr_, dst_ap=dst_ap: e.tensor_copy(
                dst_ap[:, 0:w], pr_[:, 0:w]), reads=[bpr_],
                writes=[dst_b])

        def ln_stats(src_tile, w, off, pS1, bS1, pS2, bS2, src_bufs,
                     src_is_bf):
            wspans = [(0, 512), (512, w - 512)] if w > 512 else [(0, w)]
            for ft in range(FT):
                ra = TSQ[:, (ft % 2) * STRIDE:(ft % 2) * STRIDE + w]
                rsq = TSQ[:, (2 + ft % 2) * STRIDE:(2 + ft % 2) * STRIDE + w]
                bra, brsq = bTSQ[ft % 2], bTSQ[2 + ft % 2]
                if src_is_bf:
                    sq_src = src_tile(ft)
                else:
                    S.g("vector", lambda e, ft=ft, ra=ra: e.tensor_copy(
                        ra, src_tile(ft)), reads=[src_bufs[ft]], writes=[bra])
                    sq_src = ra
                S.g("scalar", lambda e, ft=ft, rsq=rsq, sq_src=sq_src:
                    e.activation(rsq, sq_src, AF.Square),
                    reads=([src_bufs[ft]] if src_is_bf else [bra]),
                    writes=[brsq])
                s1src = src_tile(ft) if src_is_bf else ra
                s1read = src_bufs[ft] if src_is_bf else bra

                MM_LOG.append((PHASE[0], len(wspans)))

                def em1(e, ft=ft, s1src=s1src, wspans=wspans):
                    last = None
                    for (o_, wc) in wspans:
                        last = e.matmul(pS1[0:1, o_:o_ + wc], ONEB[:],
                                        s1src[:, o_:o_ + wc],
                                        start=(ft == 0), stop=(ft == FT - 1))
                    return last
                S.g("tensor", em1, reads=[s1read, bconst], writes=[bS1])

                MM_LOG.append((PHASE[0], len(wspans)))

                def em2(e, ft=ft, rsq=rsq, wspans=wspans):
                    last = None
                    for (o_, wc) in wspans:
                        last = e.matmul(pS2[0:1, o_:o_ + wc], ONEB[:],
                                        rsq[:, o_:o_ + wc],
                                        start=(ft == 0), stop=(ft == FT - 1))
                    return last
                S.g("tensor", em2, reads=[brsq, bconst], writes=[bS2])

        def rows_and_planes(pS1, bS1, pS2, bS2, w, compose_ln1=False,
                            skip_planes=False):
            R0 = T0[0:1, 0:w]
            R1 = T1[0:1, 0:w]
            R2 = TM2[0:1, 0:w]
            S.g("vector", lambda e: e.tensor_scalar_mul(R0, pS1[0:1, 0:w], 1.0 / H),
                reads=[bS1], writes=[bT0], extra=[bT0.w])
            S.g("vector", lambda e: e.tensor_scalar_mul(R1, pS2[0:1, 0:w], 1.0 / H),
                reads=[bS2], writes=[bT1], extra=[bT1.w])
            S.g("vector", lambda e: e.tensor_tensor(R2, R0, R0, ALU.mult),
                reads=[bT0], writes=[bTM2], extra=[bTM2.w])
            S.g("vector", lambda e: e.tensor_tensor(R1, R1, R2, ALU.subtract),
                reads=[bTM2], writes=[bT1], extra=[bT1.w])
            if compose_ln1:
                S.g("vector", lambda e: e.tensor_copy(R2, R1),
                    reads=[bT1], writes=[bTM2], extra=[bTM2.w])
            S.g("vector", lambda e: e.tensor_scalar_add(R1, R1, EPS),
                reads=[], writes=[bT1], extra=[bT1.w])
            S.g("scalar", lambda e: e.activation(R1, R1, AF.Ln),
                reads=[], writes=[bT1], extra=[bT1.w])
            S.g("scalar", lambda e: e.activation(R1, R1, AF.Exp, scale=-0.5),
                reads=[], writes=[bT1], extra=[bT1.w])
            S.g("vector", lambda e: e.tensor_tensor(R0, R0, R1, ALU.mult),
                reads=[bT1], writes=[bT0], extra=[bT0.w])
            rspans = [(0, 512), (512, w - 512)] if w > 512 else [(0, w)]

            def em_rep(e, pq, row, rspans=rspans):
                last = None
                for (off, wc) in rspans:
                    last = e.matmul(pq[:, off:off + wc], ONE1[0:1, :],
                                    row[:, off:off + wc], start=True, stop=True)
                return last
            def repl_planes():
                pr_, bpr_ = nsp()
                MM_LOG.append((PHASE[0], len(rspans)))
                S.g("tensor", lambda e, pr_=pr_: em_rep(e, pr_, R1),
                    reads=[bT1, bconst], writes=[bpr_])
                S.g("vector", lambda e, pr_=pr_: e.tensor_copy(
                    PLR[:, 0:w], pr_[:, 0:w]), reads=[bpr_], writes=[bPLR])
                pm_, bpm_ = nsp()
                MM_LOG.append((PHASE[0], len(rspans)))
                S.g("tensor", lambda e, pm_=pm_: em_rep(e, pm_, R0),
                    reads=[bT0, bconst], writes=[bpm_])
                S.g("vector", lambda e, pm_=pm_: e.tensor_copy(
                    PLM[:, 0:w], pm_[:, 0:w]), reads=[bpm_], writes=[bPLM])
            repl_planes()
            if compose_ln1:
                def fin():
                    S.g("vector", lambda e: e.tensor_tensor(R2, R2, R1, ALU.mult),
                        reads=[bT1], writes=[bTM2], extra=[bTM2.w])
                    S.g("vector", lambda e: e.tensor_tensor(R2, R2, R1, ALU.mult),
                        reads=[bT1], writes=[bTM2], extra=[bTM2.w])
                    S.g("vector", lambda e: e.tensor_scalar_add(R2, R2, EPS),
                        reads=[], writes=[bTM2], extra=[bTM2.w])
                    S.g("scalar", lambda e: e.activation(R2, R2, AF.Ln),
                        reads=[], writes=[bTM2], extra=[bTM2.w])
                    S.g("scalar", lambda e: e.activation(R2, R2, AF.Exp, scale=-0.5),
                        reads=[], writes=[bTM2], extra=[bTM2.w])
                    pr_, bpr_ = nsp()
                    MM_LOG.append((PHASE[0], len(rspans)))
                    S.g("tensor", lambda e, pr_=pr_: em_rep(e, pr_, R2),
                        reads=[bTM2, bconst], writes=[bpr_])
                    S.g("vector", lambda e, pr_=pr_: e.tensor_copy(
                        PLR[:, 0:w], pr_[:, 0:w]), reads=[bpr_], writes=[bPLR])
                return fin
            return None

        def moe_gu(e_, apply_cb=True, defer=False, gate_kt=False):
            def gu_mm(pq, bpq, wa, bwa):
                if gate_kt:
                    for kt in range(FT):
                        MM_LOG.append((PHASE[0], 1))
                        S.g("tensor", lambda e, kt=kt, pq=pq, wa=wa: e.matmul(
                            pq[:, 0:C], wa[:, kt * 128:(kt + 1) * 128],
                            tile(XNB[:], kt, C, ono), start=(kt == 0),
                            stop=(kt == FT - 1)),
                            reads=[bXt[kt], bwa], writes=[bpq])
                else:
                    S.g("tensor", acc_group(
                        pq, [(0, C)],
                        lambda kt, wa=wa: wa[:, kt * 128:(kt + 1) * 128],
                        lambda kt, off_, wc_: tile(XNB[:], kt, C, ono), FT),
                        reads=bX_all + [bwa], writes=[bpq])

            def pair(fp_):
                gv = fetch(dgp[e_, fp_:fp_ + 2], ("gp", e_, fp_))
                uv = fetch(dup[e_, fp_:fp_ + 2], ("up", e_, fp_))
                for oi in range(2):
                    fo = fp_ + oi
                    pg, bpg = nsp()
                    pu, bpu = nsp()
                    wga, bwga = gv[oi]
                    wua, bwua = uv[oi]
                    gu_mm(pg, bpg, wga, bwga)
                    gu_mm(pu, bpu, wua, bwua)
                    S.g("scalar", lambda e, pg=pg: e.activation(
                        TM2[:, 0:C], pg[:, 0:C], AF.Silu),
                        reads=[bpg], writes=[bTM2])
                    if apply_cb:
                        S.g("vector", lambda e, pu=pu: e.tensor_tensor(
                            TM1[:], TM2[:, 0:C], pu[:, 0:C], ALU.mult),
                            reads=[bTM2, bpu], writes=[bTM1])
                        S.g("vector", lambda e, fo=fo, e_=e_: e.tensor_tensor(
                            ACT8[:, fo * C:(fo + 1) * C], TM1[:],
                            CBp[:, e_ * C:(e_ + 1) * C], ALU.mult),
                            reads=[bTM1, bCB], writes=[bACT], extra=[bACT.w])
                    else:
                        S.g("vector", lambda e, fo=fo, pu=pu: e.tensor_tensor(
                            ACT8[:, fo * C:(fo + 1) * C], TM2[:, 0:C],
                            pu[:, 0:C], ALU.mult),
                            reads=[bTM2, bpu], writes=[bACT], extra=[bACT.w])
            if defer:
                return [(lambda fp_=fp_: pair(fp_)) for fp_ in range(0, MFT, 2)]
            for fp_ in range(0, MFT, 2):
                pair(fp_)
            return None

        def moe_down(e_, pre=None, defer=False):
            last_e = (e_ == E - 1)
            shared = {}
            units = []

            def unit(do):
                dp_ = (do // 4) * 4
                j = do % 4
                if j == 0:
                    shared[dp_] = fetch(ddp[e_, dp_:dp_ + 4], ("dp", e_, dp_))
                dv = shared[dp_]
                if pre is not None:
                    pre(do)
                pk, bpk = nsp()
                wda, bwda = dv[j]
                S.g("tensor", acc_group(
                    pk, [(0, C)],
                    lambda fl, wda=wda: wda[:, fl * 128:(fl + 1) * 128],
                    lambda fl, off_, wc_: ACT8[:, fl * C:(fl + 1) * C], MFT),
                    reads=[bACT, bwda], writes=[bpk])
                if not last_e:
                    if e_ == 0:
                        S.g("vector", lambda e, do=do, pk=pk: e.tensor_tensor(
                            tile(Bf[:], do, C), pk[:, 0:C],
                            CBp[:, 0 * C:1 * C], ALU.mult),
                            reads=[bpk, bCB], writes=[bBt[do]])
                    else:
                        S.g("vector", lambda e, do=do, pk=pk: e.tensor_tensor(
                            tile(Bf[:], do, C), tile(Bf[:], do, C),
                            pk[:, 0:C], ALU.add),
                            reads=[bpk], writes=[bBt[do]])
                else:
                    Ta, bTa, Tb, bTb = T0, bT0, T1, bT1
                    if do % 2 == 0:
                        stsem = "st0" if ci == 0 else "gst0"
                    else:
                        stsem = "st1" if ci == 0 else "gst1"
                    own0 = 2 + ci * C

                    def xr_load(k):
                        if k < FT:
                            pi = k % 3
                            S.g("scalar", lambda e, k=k, pi=pi: e.dma_start(
                                out=PO4[pi][:],
                                in_=dx[k * 128:(k + 1) * 128, own0:own0 + C]),
                                writes=[bPO[pi]], dma=True,
                                sem=f"xr{pi}" if pi < 2 else "xr2")
                    if do == 0:
                        for k in range(3):
                            xr_load(k)
                    pi = do % 3
                    S.g("vector", lambda e, do=do, pk=pk, Ta=Ta:
                        e.tensor_tensor(
                            Ta[:, 0:C], tile(Bf[:], do, C), pk[:, 0:C],
                            ALU.add),
                        reads=[bBt[do], bpk], writes=[bTa])
                    S.g("scalar", lambda e, Ta=Ta, Tb=Tb: e.activation(
                        Tb[:, 0:C], Ta[:, 0:C], AF.Silu),
                        reads=[bTa], writes=[bTb])
                    S.g("vector", lambda e, do=do, Ta=Ta, Tb=Tb:
                        e.tensor_tensor(
                            Ta[:, 0:C], Tb[:, 0:C], tile(A[:], do, C, ono),
                            ALU.mult),
                        reads=[bTb, bAt[do]], writes=[bTa], extra=[bTa.w])
                    S.g("vector", lambda e, pi=pi, Ta=Ta: e.tensor_tensor(
                        Ta[:, 0:C], Ta[:, 0:C], PO4[pi][:], ALU.add),
                        reads=[bPO[pi]], writes=[bTa], extra=[bTa.w])
                    xr_load(do + 3)
                    steng = "sync" if ci == 0 else "gpsimd"
                    S.g(steng, lambda e, do=do, Ta=Ta: e.dma_start(
                        out=dout[do * 128:(do + 1) * 128,
                                 ci * C:(ci + 1) * C],
                        in_=Ta[:, 0:C]),
                        reads=[bTa], dma=True, sem=stsem)

            if defer:
                return [(lambda do=do: unit(do)) for do in range(FT)]
            for do in range(FT):
                unit(do)
            return None

        PHASE[0] = "LN0"
        if ci == 0:
            ln_stats(lambda ft: tile(A[:], ft, cw), cw, 0, P[2], bP[2],
                     P[3], bP[3], bA_all, False)
            fin_ln1 = rows_and_planes(P[2], bP[2], P[3], bP[3], cw,
                                      compose_ln1=True)
            xeng = "vector"
            for ft in range(FT):
                S.g(xeng, lambda e, ft=ft: e.tensor_tensor(
                    tile(A[:], ft, cw), tile(A[:], ft, cw), PLR[:, 0:cw],
                    ALU.mult), reads=[bPLR], writes=[bAt[ft]])
                S.g(xeng, lambda e, ft=ft: e.tensor_tensor(
                    tile(A[:], ft, cw), tile(A[:], ft, cw), PLM[:, 0:cw],
                    ALU.subtract), reads=[bPLM], writes=[bAt[ft]])
                S.g(xeng, lambda e, ft=ft: e.tensor_copy(
                    tile(XNB[:], ft, cw), tile(A[:], ft, cw)),
                    reads=[bAt[ft]], writes=[bXt[ft]])
        else:
            def fin_ln1():
                rep_row(ROWS[0:1, 2 * C:2 * C + cw], PLR, bPLR, cw)
        fin_ln1()

        def h1_sweep_ft(ft, mix_specs):
            hw = cw if ci == 0 else 513
            if ci > 0:
                S.g("vector", lambda e, ft=ft: e.tensor_copy(
                    T0[:, 0:1], H1C[:, ((ci - 1) % 2) * FT + ft:
                                    ((ci - 1) % 2) * FT + ft + 1]),
                    reads=[bH1C], writes=[bT0])
            ho = 0 if ci == 0 else 1
            S.g("vector", lambda e, ft=ft, ho=ho: e.tensor_tensor(
                T0[:, ho:ho + cw], tile(A[:], ft, cw), PLR[:, 0:cw],
                ALU.mult), reads=[bAt[ft], bPLR], writes=[bT0],
                extra=[bT0.w])
            S.g("vector", lambda e, ft=ft, ho=ho: e.tensor_scalar(
                T0[:, ho:ho + cw], T0[:, ho:ho + cw], pp(P_G1, ft),
                pp(P_B1, ft), ALU.mult, ALU.add),
                reads=[bconst], writes=[bT0], extra=[bT0.w])
            if ci == 0:
                S.g("vector", lambda e: e.tensor_scalar_mul(
                    T0[:, 0:2], T0[:, 0:2], MSK[:]),
                    reads=[bconst], writes=[bT0], extra=[bT0.w])
            S.g("vector", lambda e, ft=ft, hw=hw: e.tensor_copy(
                H1C[:, (ci % 2) * FT + ft:(ci % 2) * FT + ft + 1],
                T0[:, hw - 1:hw]),
                reads=[bT0], writes=[bH1C])
            S.g("vector", lambda e, hw=hw: e.tensor_tensor(
                T1[:, 0:hw - 1], T0[:, 1:hw], T0[:, 0:hw - 1],
                ALU.subtract), reads=[bT0], writes=[bT1])
            for (dstap, dstb, pb) in mix_specs:
                S.g("vector", lambda e, ft=ft, dstap=dstap, pb=pb, hw=hw:
                    e.scalar_tensor_tensor(
                        tile(dstap, ft, hw - 1), T1[:, 0:hw - 1],
                        pp(pb, ft), T0[:, 0:hw - 1], ALU.mult, ALU.add),
                    reads=[bT1, bT0, bconst], writes=[dstb],
                    extra=[dstb.w])

        def h1_sweep(mix_specs):
            for ft in range(FT):
                h1_sweep_ft(ft, mix_specs)

        h1_sweep([(M1[:], bM1, P_TMR), (M2[:], bM2, P_TMV)])

        PHASE[0] = "scores"
        p3, bp3 = P[3], bP[3]
        S.g("tensor", acc_group(
            p3[0:8, :], [(0, C)], lambda kt: RTRB[:, kt * 8:(kt + 1) * 8],
            lambda kt, off_, wc_: tile(XNB[:], kt, C, ono), FT),
            reads=bX_all + [bconst], writes=[bp3])
        S.g("vector", lambda e: e.tensor_copy(SCR[:], p3[0:8, 0:C]),
            reads=[bp3], writes=[bSCR])
        PHASE[0] = "e0gu"
        moe_gu(0, apply_cb=False)
        for blk in range(4):
            pq, bpq = nsp()
            MM_LOG.append((PHASE[0], 1))
            S.g("tensor", lambda e, blk=blk, pq=pq: e.transpose(
                pq[:, 0:8], SCR[:, blk * 128:(blk + 1) * 128], ID8[:]),
                reads=[bSCR, bconst], writes=[bpq])
            st = TT8[:, 0:8]
            m1 = TT8[:, 8:9]; m2 = TT8[:, 9:10]; nm1 = TT8[:, 10:11]
            wv_ = TT8[:, 16:24]; mk2 = TT8[:, 24:32]; s2 = TT8[:, 32:40]
            S.g("vector", lambda e, pq=pq: e.tensor_copy(st, pq[:, 0:8]),
                reads=[bpq], writes=[bTT8], extra=[bTT8.w])
            S.g("vector", lambda e: e.tensor_reduce(m1, st, AX.X, ALU.max),
                reads=[bTT8], writes=[bTT8], extra=[bTT8.w])
            S.g("vector", lambda e: e.tensor_scalar(s2, st, m1, None, ALU.is_ge),
                reads=[bTT8], writes=[bTT8], extra=[bTT8.w])
            S.g("vector", lambda e: e.scalar_tensor_tensor(
                s2, s2, -3e38, st, ALU.mult, ALU.add),
                reads=[bTT8], writes=[bTT8], extra=[bTT8.w])
            S.g("vector", lambda e: e.tensor_reduce(m2, s2, AX.X, ALU.max),
                reads=[bTT8], writes=[bTT8], extra=[bTT8.w])
            S.g("vector", lambda e: e.tensor_scalar_mul(nm1, m1, -1.0),
                reads=[bTT8], writes=[bTT8], extra=[bTT8.w])
            S.g("scalar", lambda e: e.activation(wv_, st, AF.Exp, bias=nm1),
                reads=[bTT8], writes=[bTT8], extra=[bTT8.w])
            S.g("vector", lambda e: e.tensor_scalar(mk2, st, m2, None, ALU.is_ge),
                reads=[bTT8], writes=[bTT8], extra=[bTT8.w])
            S.g("vector", lambda e: e.tensor_tensor(wv_, wv_, mk2, ALU.mult),
                reads=[bTT8], writes=[bTT8], extra=[bTT8.w])
            S.g("vector", lambda e: e.tensor_tensor(m2, m2, nm1, ALU.add),
                reads=[bTT8], writes=[bTT8], extra=[bTT8.w])
            S.g("scalar", lambda e: e.activation(m2, m2, AF.Exp),
                reads=[bTT8], writes=[bTT8], extra=[bTT8.w])
            S.g("vector", lambda e: e.tensor_scalar_add(m2, m2, 1.0),
                reads=[bTT8], writes=[bTT8], extra=[bTT8.w])
            S.g("vector", lambda e: e.reciprocal(m2, m2),
                reads=[bTT8], writes=[bTT8], extra=[bTT8.w])
            S.g("vector", lambda e, blk=blk: e.tensor_scalar_mul(
                CTM[:, blk * 8:(blk + 1) * 8], wv_, m2),
                reads=[bTT8], writes=[bCTM], extra=[bCTM.w])
        for blk in range(4):
            pq, bpq = nsp()
            MM_LOG.append((PHASE[0], 1))
            S.g("tensor", lambda e, blk=blk, pq=pq: e.transpose(
                pq[0:8, 0:128], CTM[:, blk * 8:(blk + 1) * 8], ID128[:]),
                reads=[bCTM, bconst], writes=[bpq])
            S.g("vector", lambda e, blk=blk, pq=pq: e.tensor_copy(
                CFM[:, blk * 128:(blk + 1) * 128], pq[0:8, 0:128]),
                reads=[bpq], writes=[bCFM], extra=[bCFM.w])
        for e_ in range(E):
            pq, bpq = nsp()
            MM_LOG.append((PHASE[0], 1))
            S.g("tensor", lambda e, e_=e_, pq=pq: e.matmul(
                pq[:, 0:C], SEL[:, e_ * 128:(e_ + 1) * 128], CFM[:],
                start=True, stop=True),
                reads=[bCFM, bconst], writes=[bpq])
            S.g("vector", lambda e, e_=e_, pq=pq: e.tensor_copy(
                CBp[:, e_ * C:(e_ + 1) * C], pq[:, 0:C]),
                reads=[bpq], writes=[bCB], extra=[bCB.w])



        PHASE[0] = "vpass"
        for pr in range(0, FT, 2):
            vv_ = fetch(dwv[pr:pr + 2], ("wv", pr))
            for oi in range(2):
                ot = pr + oi
                pv, bpv = nsp()
                wva, bwva = vv_[oi]
                S.g("tensor", acc_group(
                    pv, spans,
                    lambda kt, wva=wva: wva[:, kt * 128:(kt + 1) * 128],
                    lambda kt, off_, wc_: tile(M2[:], kt, wc_, off_), FT),
                    reads=[bM2, bwva], writes=[bpv])
                S.g("vector", lambda e, ot=ot, pv=pv: e.tensor_copy(
                    tile(M3[:], ot, mw), pv[:, 0:mw]),
                    reads=[bpv], writes=[bM3], extra=[bM3.w])

        moe_down(0)
        PHASE[0] = "e1gu"
        moe_gu(1)

        PHASE[0] = "rpass"
        for pr in range(0, FT, 2):
            rv_ = fetch(dwr[pr:pr + 2], ("wr", pr))
            for oi in range(2):
                ot = pr + oi
                pk, bpk = nsp()
                wra, bwra = rv_[oi]
                S.g("tensor", acc_group(
                    pk, spans,
                    lambda kt, wra=wra: wra[:, kt * 128:(kt + 1) * 128],
                    lambda kt, off_, wc_: tile(M1[:], kt, wc_, off_), FT),
                    reads=[bM1, bwra], writes=[bpk])
                S.g("scalar", lambda e, pk=pk: e.activation(
                    TM2[:, 0:mw], pk[:, 0:mw], AF.Sigmoid),
                    reads=[bpk], writes=[bTM2])
                S.g("vector", lambda e, ot=ot: e.tensor_tensor(
                    tile(M3[:], ot, mw), TM2[:, 0:mw], tile(M3[:], ot, mw),
                    ALU.mult), reads=[bTM2], writes=[bM3], extra=[bM3.w])

        PHASE[0] = "e1dn_e2gu"
        moe_down(1)
        moe_gu(2)

        PHASE[0] = "wo"
        for pr in range(0, FT, 2):
            ov_ = fetch(dwo[pr:pr + 2], ("wo", pr))
            for oi in range(2):
                ot = pr + oi
                pk, bpk = nsp()
                woa, bwoa = ov_[oi]
                S.g("tensor", acc_group(
                    pk, spans,
                    lambda kt, woa=woa: woa[:, kt * 128:(kt + 1) * 128],
                    lambda kt, off_, wc_: tile(M3[:], kt, wc_, off_), FT),
                    reads=[bM3, bwoa], writes=[bpk])
                S.g("vector", lambda e, ot=ot, pk=pk: e.tensor_tensor(
                    tile(A[:], ot, mw, ox1), tile(A[:], ot, mw, ox1),
                    pk[:, 0:mw], ALU.add),
                    reads=[bpk], writes=[bAt[ot]])

        PHASE[0] = "LN2"
        ln_stats(lambda ft: tile(A[:], ft, mw, ox1), mw, ox1, P[2], bP[2],
                 P[3], bP[3], bA_all, False)
        rows_and_planes(P[2], bP[2], P[3], bP[3], mw)

        def h2_ft(ft):
            if ci > 0:
                S.g("vector", lambda e, ft=ft: e.tensor_copy(
                    T0[:, 0:1], H2C[:, ft:ft + 1]),
                    reads=[bH2C], writes=[bT0])
            ho = 0 if ci == 0 else 1
            S.g("vector", lambda e, ft=ft, ho=ho: e.tensor_tensor(
                T0[:, ho:ho + mw], tile(A[:], ft, mw, ox1), PLR[:, 0:mw],
                ALU.mult), reads=[bAt[ft], bPLR], writes=[bT0], extra=[bT0.w])
            S.g("vector", lambda e, ho=ho: e.tensor_tensor(
                T0[:, ho:ho + mw], T0[:, ho:ho + mw], PLM[:, 0:mw],
                ALU.subtract), reads=[bPLM], writes=[bT0], extra=[bT0.w])
            S.g("vector", lambda e, ft=ft, ho=ho: e.tensor_scalar(
                T0[:, ho:ho + mw], T0[:, ho:ho + mw], pp(P_G2, ft),
                pp(P_B2, ft), ALU.mult, ALU.add),
                reads=[bconst], writes=[bT0], extra=[bT0.w])
            if ci == 0:
                S.g("vector", lambda e: e.tensor_scalar_mul(
                    T0[:, 0:1], T0[:, 0:1], MSK[:]),
                    reads=[bconst], writes=[bT0], extra=[bT0.w])
            S.g("vector", lambda e, ft=ft: e.tensor_copy(
                H2C[:, ft:ft + 1], T0[:, 512:513]),
                reads=[bT0], writes=[bH2C])
            S.g("vector", lambda e: e.tensor_tensor(
                T1[:, 0:C], T0[:, 1:513], T0[:, 0:C], ALU.subtract),
                reads=[bT0], writes=[bT1])
            S.g("vector", lambda e, ft=ft: e.scalar_tensor_tensor(
                tile(M1[:], ft, C), T1[:, 0:C], pp(P_CMK, ft),
                T0[:, 0:C], ALU.mult, ALU.add),
                reads=[bT1, bT0, bconst], writes=[bM1], extra=[bM1.w])
            S.g("vector", lambda e, ft=ft: e.scalar_tensor_tensor(
                tile(M3[:], ft, C), T1[:, 0:C], pp(P_CMR, ft),
                T0[:, 0:C], ALU.mult, ALU.add),
                reads=[bT1, bT0, bconst], writes=[bM3], extra=[bM3.w])

        PHASE[0] = "e2dn_h2"
        moe_down(2, pre=h2_ft)
        PHASE[0] = "e3gu"
        moe_gu(3)

        PHASE[0] = "wcr"
        for pr in range(0, FT, 2):
            cv_ = fetch(dwcr[pr:pr + 2], ("wcr", pr))
            for oi in range(2):
                ot = pr + oi
                pk, bpk = nsp()
                wca, bwca = cv_[oi]
                S.g("tensor", acc_group(
                    pk, [(0, C)],
                    lambda kt, wca=wca: wca[:, kt * 128:(kt + 1) * 128],
                    lambda kt, off_, wc_: tile(M3[:], kt, C), FT),
                    reads=[bM3, bwca], writes=[bpk])
                S.g("scalar", lambda e, pk=pk: e.activation(
                    TM2[:, 0:C], pk[:, 0:C], AF.Sigmoid),
                    reads=[bpk], writes=[bTM2])
                S.g("vector", lambda e, ot=ot: e.tensor_copy(
                    tile(M2[:], ot, C), TM2[:, 0:C]),
                    reads=[bTM2], writes=[bM2], extra=[bM2.w])

        PHASE[0] = "e3dn"
        moe_down(3)

        es_units = None
        if ci < NCH - 1:
            xoff1 = 2 + (ci + 1) * C

            def es_unit(u):
                for ft in (2 * u, 2 * u + 1):
                    sl = ft % 2
                    ra = TSQ[:, sl * STRIDE:sl * STRIDE + C]
                    rsq = TSQ[:, (2 + sl) * STRIDE:(2 + sl) * STRIDE + C]
                    bra, brsq = bTSQ[sl], bTSQ[2 + sl]
                    S.g("gpsimd", lambda e, ft=ft, ra=ra: e.dma_start(
                        out=ra, in_=dx[ft * 128:(ft + 1) * 128,
                                       xoff1:xoff1 + C]),
                        writes=[bra], dma=True, sem=f"xa{sl}")
                    S.g("scalar", lambda e, ra=ra, rsq=rsq: e.activation(
                        rsq, ra, AF.Square), reads=[bra], writes=[brsq])
                    MM_LOG.append((PHASE[0], 1))
                    S.g("tensor", lambda e, ft=ft, ra=ra: e.matmul(
                        P[2][0:1, 0:C], ONEB[:], ra, start=(ft == 0),
                        stop=(ft == FT - 1)),
                        reads=[bra, bconst], writes=[bP[2]])
                    MM_LOG.append((PHASE[0], 1))
                    S.g("tensor", lambda e, ft=ft, rsq=rsq: e.matmul(
                        P[3][0:1, 0:C], ONEB[:], rsq, start=(ft == 0),
                        stop=(ft == FT - 1)),
                        reads=[brsq, bconst], writes=[bP[3]])
            es_units = [lambda u=u: es_unit(u) for u in range(8)]

        PHASE[0] = "wgrp"
        for grp in range(4):
            PHASE[0] = f"e{4+grp}gu"
            gu_units = moe_gu(4 + grp, defer=True)
            PHASE[0] = f"wkey{grp}"
            for fp_ in range(0, 16, 2):
                fo0 = grp * 16 + fp_
                if fp_ % 4 == 0:
                    gu_units[fp_ // 4]()
                kyv = fetch(dwkey[fo0:fo0 + 2], ("wkey", fo0))
                for oi in range(2):
                    fl = fp_ + oi
                    pk, bpk = nsp()
                    wya, bwya = kyv[oi]
                    S.g("tensor", acc_group(
                        pk, [(0, C)],
                        lambda kt, wya=wya: wya[:, kt * 128:(kt + 1) * 128],
                        lambda kt, off_, wc_: tile(M1[:], kt, C), FT),
                        reads=[bM1, bwya], writes=[bpk])
                    S.g("scalar", lambda e, fl=fl, pk=pk: e.activation(
                        tile(M3[:], fl, C), pk[:, 0:C], AF.Silu),
                        reads=[bpk], writes=[bM3], extra=[bM3.w])
            PHASE[0] = f"wval{grp}"
            if grp < 3:
                units = moe_down(4 + grp, defer=True)
                if grp == 2 and es_units is not None:
                    resv[0] = True
            else:
                PHASE[0] = "finale"
                units = moe_down(7, defer=True)
                if es_units is not None:
                    PHASE[0] = "replnext"
                    rep_row(ROWS[0:1, C:2 * C], PLR, bPLR, C)
                    rep_row(ROWS[0:1, 0:C], PLM, bPLM, C)
                PHASE[0] = f"wval{grp}"
            for op_ in range(0, 16, 2):
                if grp == 2 and es_units is not None:
                    es_units[op_ // 2]()
                wvv = fetch(dwval[op_:op_ + 2, :, grp * 2048:(grp + 1) * 2048], ("wval", grp, op_))
                for oi in range(2):
                    ov = op_ + oi
                    pk, bpk = nsp()
                    wvla, bwvla = wvv[oi]
                    S.g("tensor", acc_group(
                        pk, [(0, C)],
                        lambda fl, wvla=wvla: wvla[:, fl * 128:(fl + 1) * 128],
                        lambda fl, off_, wc_: tile(M3[:], fl, C), FT),
                        reads=[bM3, bwvla], writes=[bpk])
                    S.g("vector", lambda e, ov=ov, pk=pk: e.tensor_tensor(
                        TM2[:, 0:C], pk[:, 0:C], tile(M2[:], ov, C), ALU.mult),
                        reads=[bpk, bM2], writes=[bTM2])
                    S.g("vector", lambda e, ov=ov: e.tensor_tensor(
                        tile(A[:], ov, C, ono), tile(A[:], ov, C, ono),
                        TM2[:, 0:C], ALU.add),
                        reads=[bTM2], writes=[bAt[ov]])
                    units[ov]()
                    if grp == 3 and es_units is not None:
                        S.g("gpsimd", lambda e, ov=ov: e.dma_start(
                            out=tile(A[:], ov, C),
                            in_=dx[ov * 128:(ov + 1) * 128, xoff1:xoff1 + C]),
                            writes=[bAt[ov]], dma=True, sem=f"lx{ov}")
                        S.g("gpsimd", lambda e, ov=ov: e.tensor_tensor(
                            tile(A[:], ov, C), tile(A[:], ov, C),
                            PLR[:, 0:C], ALU.mult),
                            reads=[bPLR], writes=[bAt[ov]])
                        S.g("gpsimd", lambda e, ov=ov: e.tensor_tensor(
                            tile(A[:], ov, C), tile(A[:], ov, C),
                            PLM[:, 0:C], ALU.subtract),
                            reads=[bPLM], writes=[bAt[ov]])
                        S.g("gpsimd", lambda e, ov=ov: e.tensor_copy(
                            tile(XNB[:], ov, C), tile(A[:], ov, C)),
                            reads=[bAt[ov]], writes=[bXt[ov]])

            if grp == 2 and es_units is not None:
                R0e = ROWS[0:1, 0:C]
                R1e = ROWS[0:1, C:2 * C]
                R2e = ROWS[0:1, 2 * C:3 * C]

                def rv(fn, rd=()):
                    S.g("vector", fn,
                        reads=[bP[2], bP[3]] if rd == "p" else [],
                        writes=[bROWS], extra=[bROWS.w])

                def rs(fn):
                    S.g("scalar", fn, reads=[], writes=[bROWS], extra=[bROWS.w])
                rv(lambda e: e.tensor_scalar_mul(R0e, P[2][0:1, 0:C], 1.0 / H),
                   rd="p")
                rv(lambda e: e.tensor_scalar_mul(R1e, P[3][0:1, 0:C], 1.0 / H),
                   rd="p")
                rv(lambda e: e.tensor_tensor(R2e, R0e, R0e, ALU.mult))
                rv(lambda e: e.tensor_tensor(R1e, R1e, R2e, ALU.subtract))
                rv(lambda e: e.tensor_copy(R2e, R1e))
                rv(lambda e: e.tensor_scalar_add(R1e, R1e, EPS))
                rs(lambda e: e.activation(R1e, R1e, AF.Ln))
                rs(lambda e: e.activation(R1e, R1e, AF.Exp, scale=-0.5))
                rv(lambda e: e.tensor_tensor(R0e, R0e, R1e, ALU.mult))
                rv(lambda e: e.tensor_tensor(R2e, R2e, R1e, ALU.mult))
                rv(lambda e: e.tensor_tensor(R2e, R2e, R1e, ALU.mult))
                rv(lambda e: e.tensor_scalar_add(R2e, R2e, EPS))
                rs(lambda e: e.activation(R2e, R2e, AF.Ln))
                rs(lambda e: e.activation(R2e, R2e, AF.Exp, scale=-0.5))
                resv[0] = False


    for ci in range(NCH):
        PHASE[0] = f"c{ci}"
        emit_chunk(ci)
    assert len(blk_map) == NBLK, f"block count {len(blk_map)} != NBLK {NBLK}"

    nst = {k: S.count[k] for k in ("st0", "st1", "gst0", "gst1")}

    sem_es = ExitStack()
    sems = {}
    for name in ENGINES + DMA_SEMS:
        sems[name] = sem_es.enter_context(nc.semaphore(f"s_{name}"))
    with nc.Block() as block:

        @block.tensor
        def _(e):
            S.emit_engine(e, "tensor", sems)

        @block.vector
        def _(e):
            S.emit_engine(e, "vector", sems)

        @block.scalar
        def _(e):
            S.emit_engine(e, "scalar", sems)

        @block.sync
        def _(e):
            S.emit_engine(e, "sync", sems)
            for k_ in ("st0", "st1", "gst0", "gst1"):
                e.wait_ge(sems[k_], nst[k_])

        @block.gpsimd
        def _(e):
            S.emit_engine(e, "gpsimd", sems)

    sem_es.close()
    es.close()
    return nc


_NC_CACHE = [None]


def kernel(**inputs):
    per_core = _prep_inputs(inputs)
    if _NC_CACHE[0] is None:
        _NC_CACHE[0] = build_program()
    nc = _NC_CACHE[0]
    res = run_bass_kernel_spmd(nc, per_core, core_ids=list(range(CORES)))
    out = np.empty((B, T, H), dtype=np.float32)
    for c in range(CORES):
        b, half = c // 2, c % 2
        out[b, half * TOK:(half + 1) * TOK, :] = res.results[c]["out"].T
    return out


# version 22 (final)
# speedup vs baseline: 1.1607x; 1.0484x over previous
import numpy as np
from contextlib import ExitStack

import concourse.bass as bass
import concourse.mybir as mybir
from concourse.bass_utils import run_bass_kernel_spmd

F32 = mybir.dt.float32
BF16 = mybir.dt.bfloat16
AF = mybir.ActivationFunctionType
ALU = mybir.AluOpType
AX = mybir.AxisListType

H = 2048
FT = 16
B = 4
T = 4096
CORES = 8
TOK = 2048
C = 512
NCH = 4
KH = 8192
KHT = 64
MF = 1024
MFT = 8
E = 8
STRIDE = 516
EPS = 1e-5
NEG = -1e38

ENGINES = ("tensor", "vector", "scalar", "sync", "gpsimd")
PHASE = ["?"]
MM_LOG = []


class Buf:
    __slots__ = ("name", "w", "r")

    def __init__(self, name):
        self.name = name
        self.w = None
        self.r = []


DMA_SEMS = ("cst", "wcst", "lx0", "lx1", "lx2", "lx3", "lx4", "lx5", "lx6", "lx7", "lx8", "lx9", "lx10", "lx11", "lx12", "lx13", "lx14", "lx15", "wd0", "wd1", "wd2", "hd0", "hd1", "hd2", "hd3", "hd4", "hd5", "wb0", "wb1", "wb2", "xr0", "xr1", "xr2", "xa0", "xa1", "st0", "st1", "gst0", "gst1")


class Sched:
    def __init__(self):
        self.items = {e: [] for e in ENGINES}
        self.count = {s: 0 for s in ENGINES + DMA_SEMS}

    def g(self, eng, emit, reads=(), writes=(), extra=(), dma=False, sem=None):
        sem_name = sem or eng
        deps = []
        for b in reads:
            if b.w is not None:
                deps.append(b.w)
        for b in writes:
            if b.w is not None:
                deps.append(b.w)
            deps.extend(b.r)
        deps.extend(t for t in extra if t is not None)
        waits = list(deps)
        inc = 16 if dma else 1
        self.count[sem_name] += inc
        tok = (sem_name, self.count[sem_name])
        self.items[eng].append((waits, emit, inc, sem_name))
        for b in reads:
            b.r.append(tok)
        for b in writes:
            b.w = tok
            b.r = []
        return tok

    def emit_engine(self, e, eng_name, sems):
        waited = {}
        for waits, emit, inc, sem_name in self.items[eng_name]:
            need = {}
            for (pe, val) in waits:
                if val > need.get(pe, 0):
                    need[pe] = val
            for pe, val in need.items():
                if waited.get(pe, 0) < val:
                    e.wait_ge(sems[pe], val)
                    waited[pe] = val
            emit(e).then_inc(sems[sem_name], inc)


def _host_tile_lhsT(w_io, kt, ot):
    a = w_io.reshape(kt, 128, ot, 128).transpose(2, 1, 0, 3)
    return np.ascontiguousarray(a.reshape(ot, 128, kt * 128))


def _prep_inputs(I):
    d = {}
    d["wk"] = _host_tile_lhsT(np.ascontiguousarray(I["L0_attn_Wk"].T), FT, FT)
    d["wv"] = _host_tile_lhsT(np.ascontiguousarray(I["L0_attn_Wv"].T), FT, FT)
    d["wr"] = _host_tile_lhsT(np.ascontiguousarray(I["L0_attn_Wr"].T), FT, FT)
    d["wo"] = _host_tile_lhsT(np.ascontiguousarray(I["L0_attn_Wo"].T), FT, FT)
    d["wcr"] = _host_tile_lhsT(np.ascontiguousarray(I["L0_attn_Wcr"].T), FT, FT)
    d["wkey"] = _host_tile_lhsT(np.ascontiguousarray(I["L0_attn_Wkey"].T), FT, KHT)
    d["wval"] = _host_tile_lhsT(np.ascontiguousarray(I["L0_attn_Wval"].T), KHT, FT)
    d["gp"] = np.stack([_host_tile_lhsT(np.asarray(I["L0_ffn_gate_proj"][e]), FT, MFT)
                        for e in range(E)])
    d["up"] = np.stack([_host_tile_lhsT(np.asarray(I["L0_ffn_up_proj"][e]), FT, MFT)
                        for e in range(E)])
    d["dp"] = np.stack([_host_tile_lhsT(np.asarray(I["L0_ffn_down_proj"][e]), MFT, FT)
                        for e in range(E)])
    rt = np.asarray(I["L0_ffn_router_w"]).reshape(FT, 128, E).transpose(1, 0, 2)
    d["rtr"] = np.ascontiguousarray(rt.reshape(128, FT * E))
    vecs = [I["L0_attn_ln1_g"], I["L0_attn_ln1_b"], I["L0_attn_ln2_g"],
            I["L0_attn_ln2_b"], I["L0_attn_tm_first"],
            np.asarray(I["L0_attn_tm_mk"]).reshape(H),
            np.asarray(I["L0_attn_tm_mv"]).reshape(H),
            np.asarray(I["L0_attn_tm_mr"]).reshape(H),
            np.asarray(I["L0_attn_cm_mk"]).reshape(H),
            np.asarray(I["L0_attn_cm_mr"]).reshape(H)]
    d["pp"] = np.concatenate(
        [np.ascontiguousarray(np.asarray(v, dtype=np.float32).reshape(FT, 128).T)
         for v in vecs], axis=1)
    d["id8"] = np.eye(8, dtype=np.float32)
    d["id128"] = np.eye(128, dtype=np.float32)
    sel = np.zeros((8, 8 * 128), dtype=np.float32)
    for e in range(E):
        sel[e, e * 128:(e + 1) * 128] = 1.0
    d["sel"] = sel
    for k in d:
        d[k] = np.ascontiguousarray(d[k], dtype=np.float32)

    x = np.asarray(I["L0_attn_x"], dtype=np.float32)
    per_core = []
    for c in range(CORES):
        b, half = c // 2, c % 2
        if half == 0:
            xs = np.zeros((TOK + 2, H), dtype=np.float32)
            xs[2:] = x[b, 0:TOK]
            m = 0.0
        else:
            xs = x[b, TOK - 2:2 * TOK]
            m = 1.0
        cd = dict(d)
        cd["x"] = np.ascontiguousarray(xs.T)
        cd["msk"] = np.full((128, 1), m, dtype=np.float32)
        per_core.append(cd)
    return per_core


P_G1, P_B1, P_G2, P_B2, P_TMF, P_TMK, P_TMV, P_TMR, P_CMK, P_CMR = \
    [i * 16 for i in range(10)]


def build_program():
    nc = bass.Bass("TRN2", target_bir_lowering=False)
    S = Sched()

    dx = nc.dram_tensor("x", [H, TOK + 2], F32, kind="ExternalInput").ap()
    dwk = nc.dram_tensor("wk", [FT, 128, H], F32, kind="ExternalInput").ap()
    dwv = nc.dram_tensor("wv", [FT, 128, H], F32, kind="ExternalInput").ap()
    dwr = nc.dram_tensor("wr", [FT, 128, H], F32, kind="ExternalInput").ap()
    dwo = nc.dram_tensor("wo", [FT, 128, H], F32, kind="ExternalInput").ap()
    dwcr = nc.dram_tensor("wcr", [FT, 128, H], F32, kind="ExternalInput").ap()
    dwkey = nc.dram_tensor("wkey", [KHT, 128, H], F32, kind="ExternalInput").ap()
    dwval = nc.dram_tensor("wval", [FT, 128, KH], F32, kind="ExternalInput").ap()
    dgp = nc.dram_tensor("gp", [E, MFT, 128, H], F32, kind="ExternalInput").ap()
    dup = nc.dram_tensor("up", [E, MFT, 128, H], F32, kind="ExternalInput").ap()
    ddp = nc.dram_tensor("dp", [E, FT, 128, MF], F32, kind="ExternalInput").ap()
    drtr = nc.dram_tensor("rtr", [128, 128], F32, kind="ExternalInput").ap()
    dpp = nc.dram_tensor("pp", [128, 160], F32, kind="ExternalInput").ap()
    did8 = nc.dram_tensor("id8", [8, 8], F32, kind="ExternalInput").ap()
    did128 = nc.dram_tensor("id128", [128, 128], F32, kind="ExternalInput").ap()
    dsel = nc.dram_tensor("sel", [8, 8 * 128], F32, kind="ExternalInput").ap()
    dmsk = nc.dram_tensor("msk", [128, 1], F32, kind="ExternalInput").ap()
    dout = nc.dram_tensor("out", [H, TOK], F32, kind="ExternalOutput").ap()
    NBLK = 192
    dwpk = nc.dram_tensor("wpk", [NBLK, 128, 4096], BF16, kind="Internal").ap()

    es = ExitStack()
    sb = lambda n, sh, dt: es.enter_context(nc.sbuf_tensor(n, sh, dt))
    psd = lambda n, sh: es.enter_context(nc.psum_tensor(n, sh, F32))

    A = sb("A", [128, FT * STRIDE], F32)
    Bf = sb("Bf", [128, FT * STRIDE], F32)
    XNB = sb("XNB", [128, FT * STRIDE], BF16)
    M1 = sb("M1", [128, FT * STRIDE], BF16)
    M2 = sb("M2", [128, FT * STRIDE], BF16)
    M3 = sb("M3", [128, FT * STRIDE], BF16)
    ACT8 = sb("ACT8", [128, MFT * C], BF16)
    CBp = sb("CBp", [128, E * C], BF16)
    PLR = sb("PLR", [128, STRIDE], F32)
    PLM = sb("PLM", [128, STRIDE], F32)
    T0 = sb("T0", [128, STRIDE], F32)
    T1 = sb("T1", [128, STRIDE], F32)
    TM1 = sb("TM1", [128, C], BF16)
    TM2 = sb("TM2", [128, STRIDE], F32)
    ROWS = sb("ROWS", [128, 3 * C], F32)
    TSQ = sb("TSQ", [128, 4 * STRIDE], BF16)
    PO4 = [sb(f"PO{i}", [128, C], F32) for i in range(3)]
    SLOTB = sb("SLOTB", [128, 6 * 2048], BF16)
    H1C = sb("H1C", [128, 2 * FT], F32)
    H2C = sb("H2C", [128, FT], F32)
    RTRB = sb("RTRB", [128, 128], BF16)
    PPB = sb("PPB", [128, 160], F32)
    ID8 = sb("ID8", [8, 8], F32)
    ID128 = sb("ID128", [128, 128], F32)
    SEL = sb("SEL", [8, 8 * 128], BF16)
    MSK = sb("MSK", [128, 1], F32)
    ONEB = sb("ONEB", [128, 1], BF16)
    ONE1 = sb("ONE1", [1, 128], F32)
    SCR = sb("SCR", [8, C], F32)
    CFM = sb("CFM", [8, C], BF16)
    CTM = sb("CTM", [128, 32], F32)
    TT8 = sb("TT8", [128, 48], F32)

    P = [psd(f"p{i}", [128, 520]) for i in range(4)]

    bAt = [Buf(f"A{i}") for i in range(FT)]
    bBt = [Buf(f"B{i}") for i in range(FT)]
    bXt = [Buf(f"X{i}") for i in range(FT)]
    bM1, bM2, bM3 = Buf("M1"), Buf("M2"), Buf("M3")
    bACT, bCB = Buf("ACT"), Buf("CB")
    bPLR, bPLM = Buf("PLR"), Buf("PLM")
    bT0, bT1 = Buf("T0"), Buf("T1")
    bTM1, bTM2 = Buf("TM1"), Buf("TM2")
    bROWS = Buf("ROWS")
    bTSQ = [Buf(f"TSQ{i}") for i in range(4)]
    bPO = [Buf(f"PO{i}") for i in range(3)]
    bSL = [Buf(f"SL{i}") for i in range(6)]
    bH1C, bH2C = Buf("H1C"), Buf("H2C")
    bSCR, bCFM, bCTM, bTT8 = Buf("SCR"), Buf("CFM"), Buf("CTM"), Buf("TT8")
    bP = [Buf(f"p{i}") for i in range(4)]
    bconst = Buf("const")

    def tile(bufap, ft, w, off=0):
        return bufap[:, ft * STRIDE + off: ft * STRIDE + off + w]

    S.g("gpsimd", lambda e: e.dma_start(out=RTRB[:], in_=drtr),
        writes=[bconst], dma=True, sem="wcst")
    S.g("sync", lambda e: e.dma_start(out=PPB[:], in_=dpp), writes=[bconst],
        dma=True, sem="cst")
    S.g("sync", lambda e: e.dma_start(out=ID8[:], in_=did8), writes=[bconst],
        dma=True, sem="cst")
    S.g("sync", lambda e: e.dma_start(out=ID128[:], in_=did128),
        writes=[bconst], dma=True, sem="cst")
    S.g("gpsimd", lambda e: e.dma_start(out=SEL[:], in_=dsel), writes=[bconst],
        dma=True, sem="wcst")
    S.g("sync", lambda e: e.dma_start(out=MSK[:], in_=dmsk), writes=[bconst],
        dma=True, sem="cst")
    S.g("vector", lambda e: e.memset(ONEB[:], 1.0), writes=[bconst])
    S.g("vector", lambda e: e.memset(ONE1[:], 1.0), writes=[bconst])

    slot_i = [0]
    hslot_i = [0]
    blk_map = {}
    conv = set()
    bWPK = [Buf(f"wpk{j}") for j in range(NBLK)]

    def fetch(dram_ap, key):
        a, _, cc = dram_ap.shape
        assert a * cc == 4096
        if key not in blk_map:
            blk_map[key] = len(blk_map)
        blk = blk_map[key]
        if key not in conv:
            conv.add(key)
            p = slot_i[0] % 3
            slot_i[0] += 1
            h0, h1_ = 2 * p, 2 * p + 1
            region = SLOTB[:, h0 * 2048:(h1_ + 1) * 2048]

            def em(e, d=dram_ap, region=region, cc=cc):
                dst = region.rearrange("p (a c) -> p a c", c=cc)
                return e.dma_start(out=dst, in_=d.transpose([1, 0, 2]))
            S.g("gpsimd", em, writes=[bSL[h0], bSL[h1_]], dma=True, sem=f"wd{p}")
            S.g("sync", lambda e, region=region, blk=blk: e.dma_start(
                out=dwpk[blk, :, :], in_=region),
                reads=[bSL[h0], bSL[h1_]], writes=[bWPK[blk]], dma=True,
                sem=f"wb{p}")
            out = []
            for ai in range(a):
                base = h0 * 2048 + ai * cc
                out.append((SLOTB[:, base:base + cc], bSL[h0 + (ai * cc) // 2048]))
            return out
        else:
            out = []
            for half in range(2):
                j = hslot_i[0] % 6
                hslot_i[0] += 1
                S.g("sync", lambda e, j=j, blk=blk, half=half: e.dma_start(
                    out=SLOTB[:, j * 2048:(j + 1) * 2048],
                    in_=dwpk[blk, :, half * 2048:(half + 1) * 2048]),
                    reads=[bWPK[blk]], writes=[bSL[j]], dma=True, sem=f"hd{j}")
                for ai in range(a // 2):
                    out.append((SLOTB[:, j * 2048 + ai * cc:
                                      j * 2048 + (ai + 1) * cc], bSL[j]))
            return out

    pp = lambda base, ft: PPB[:, base + ft: base + ft + 1]

    def acc_group(pk, spans, lhs_of_kt, rhs_of_kt, nkt):
        MM_LOG.append((PHASE[0], len(spans) * nkt))

        def em(e):
            last = None
            for (off, wc) in spans:
                for kt in range(nkt):
                    last = e.matmul(pk[:, off:off + wc], lhs_of_kt(kt),
                                    rhs_of_kt(kt, off, wc),
                                    start=(kt == 0), stop=(kt == nkt - 1))
            return last
        return em

    def emit_chunk(ci):
        cw = 514 if ci == 0 else 512
        mw = 513 if ci == 0 else 512
        xoff = 0 if ci == 0 else 2 + ci * C
        ono = 2 if ci == 0 else 0
        ox1 = 1 if ci == 0 else 0
        spans = [(0, 512), (512, mw - 512)] if mw > 512 else [(0, mw)]
        sspans = [(0, 512), (512, cw - 512)] if cw > 512 else [(0, cw)]
        pcnt = [0]
        resv = [False]

        def nsp():
            while True:
                i = pcnt[0] % 4
                pcnt[0] += 1
                if not (resv[0] and i >= 2):
                    return P[i], bP[i]

        bA_all = list(bAt)
        bB_all = list(bBt)
        bX_all = list(bXt)

        PHASE[0] = "P0load"
        for ft in range(FT):
            S.g("sync", lambda e, ft=ft: e.dma_start(
                out=tile(A[:], ft, cw),
                in_=dx[ft * 128:(ft + 1) * 128, xoff:xoff + cw]),
                writes=[bAt[ft]], dma=True, sem=f"lx{ft}")

        def ln_stats(src_tile, w, off, pS1, bS1, pS2, bS2, src_bufs,
                     src_is_bf):
            wspans = [(0, 512), (512, w - 512)] if w > 512 else [(0, w)]
            for ft in range(FT):
                ra = TSQ[:, (ft % 2) * STRIDE:(ft % 2) * STRIDE + w]
                rsq = TSQ[:, (2 + ft % 2) * STRIDE:(2 + ft % 2) * STRIDE + w]
                bra, brsq = bTSQ[ft % 2], bTSQ[2 + ft % 2]
                if src_is_bf:
                    sq_src = src_tile(ft)
                else:
                    S.g("vector", lambda e, ft=ft, ra=ra: e.tensor_copy(
                        ra, src_tile(ft)), reads=[src_bufs[ft]], writes=[bra])
                    sq_src = ra
                S.g("scalar", lambda e, ft=ft, rsq=rsq, sq_src=sq_src:
                    e.activation(rsq, sq_src, AF.Square),
                    reads=([src_bufs[ft]] if src_is_bf else [bra]),
                    writes=[brsq])
                s1src = src_tile(ft) if src_is_bf else ra
                s1read = src_bufs[ft] if src_is_bf else bra

                MM_LOG.append((PHASE[0], len(wspans)))

                def em1(e, ft=ft, s1src=s1src, wspans=wspans):
                    last = None
                    for (o_, wc) in wspans:
                        last = e.matmul(pS1[0:1, o_:o_ + wc], ONEB[:],
                                        s1src[:, o_:o_ + wc],
                                        start=(ft == 0), stop=(ft == FT - 1))
                    return last
                S.g("tensor", em1, reads=[s1read, bconst], writes=[bS1])

                MM_LOG.append((PHASE[0], len(wspans)))

                def em2(e, ft=ft, rsq=rsq, wspans=wspans):
                    last = None
                    for (o_, wc) in wspans:
                        last = e.matmul(pS2[0:1, o_:o_ + wc], ONEB[:],
                                        rsq[:, o_:o_ + wc],
                                        start=(ft == 0), stop=(ft == FT - 1))
                    return last
                S.g("tensor", em2, reads=[brsq, bconst], writes=[bS2])

        def rows_and_planes(pS1, bS1, pS2, bS2, w, compose_ln1=False,
                            skip_planes=False):
            R0 = T0[0:1, 0:w]
            R1 = T1[0:1, 0:w]
            R2 = TM2[0:1, 0:w]
            S.g("vector", lambda e: e.tensor_scalar_mul(R0, pS1[0:1, 0:w], 1.0 / H),
                reads=[bS1], writes=[bT0], extra=[bT0.w])
            S.g("vector", lambda e: e.tensor_scalar_mul(R1, pS2[0:1, 0:w], 1.0 / H),
                reads=[bS2], writes=[bT1], extra=[bT1.w])
            S.g("vector", lambda e: e.tensor_tensor(R2, R0, R0, ALU.mult),
                reads=[bT0], writes=[bTM2], extra=[bTM2.w])
            S.g("vector", lambda e: e.tensor_tensor(R1, R1, R2, ALU.subtract),
                reads=[bTM2], writes=[bT1], extra=[bT1.w])
            if compose_ln1:
                S.g("vector", lambda e: e.tensor_copy(R2, R1),
                    reads=[bT1], writes=[bTM2], extra=[bTM2.w])
            S.g("vector", lambda e: e.tensor_scalar_add(R1, R1, EPS),
                reads=[], writes=[bT1], extra=[bT1.w])
            S.g("scalar", lambda e: e.activation(R1, R1, AF.Ln),
                reads=[], writes=[bT1], extra=[bT1.w])
            S.g("scalar", lambda e: e.activation(R1, R1, AF.Exp, scale=-0.5),
                reads=[], writes=[bT1], extra=[bT1.w])
            S.g("vector", lambda e: e.tensor_tensor(R0, R0, R1, ALU.mult),
                reads=[bT1], writes=[bT0], extra=[bT0.w])
            rspans = [(0, 512), (512, w - 512)] if w > 512 else [(0, w)]

            def em_rep(e, pq, row, rspans=rspans):
                last = None
                for (off, wc) in rspans:
                    last = e.matmul(pq[:, off:off + wc], ONE1[:],
                                    row[:, off:off + wc], start=True, stop=True)
                return last
            def repl_planes():
                pr_, bpr_ = nsp()
                MM_LOG.append((PHASE[0], len(rspans)))
                S.g("tensor", lambda e, pr_=pr_: em_rep(e, pr_, R1),
                    reads=[bT1, bconst], writes=[bpr_])
                S.g("vector", lambda e, pr_=pr_: e.tensor_copy(
                    PLR[:, 0:w], pr_[:, 0:w]), reads=[bpr_], writes=[bPLR])
                pm_, bpm_ = nsp()
                MM_LOG.append((PHASE[0], len(rspans)))
                S.g("tensor", lambda e, pm_=pm_: em_rep(e, pm_, R0),
                    reads=[bT0, bconst], writes=[bpm_])
                S.g("vector", lambda e, pm_=pm_: e.tensor_copy(
                    PLM[:, 0:w], pm_[:, 0:w]), reads=[bpm_], writes=[bPLM])
            repl_planes()
            if compose_ln1:
                def fin():
                    S.g("vector", lambda e: e.tensor_tensor(R2, R2, R1, ALU.mult),
                        reads=[bT1], writes=[bTM2], extra=[bTM2.w])
                    S.g("vector", lambda e: e.tensor_tensor(R2, R2, R1, ALU.mult),
                        reads=[bT1], writes=[bTM2], extra=[bTM2.w])
                    S.g("vector", lambda e: e.tensor_scalar_add(R2, R2, EPS),
                        reads=[], writes=[bTM2], extra=[bTM2.w])
                    S.g("scalar", lambda e: e.activation(R2, R2, AF.Ln),
                        reads=[], writes=[bTM2], extra=[bTM2.w])
                    S.g("scalar", lambda e: e.activation(R2, R2, AF.Exp, scale=-0.5),
                        reads=[], writes=[bTM2], extra=[bTM2.w])
                    pr_, bpr_ = nsp()
                    MM_LOG.append((PHASE[0], len(rspans)))
                    S.g("tensor", lambda e, pr_=pr_: em_rep(e, pr_, R2),
                        reads=[bTM2, bconst], writes=[bpr_])
                    S.g("vector", lambda e, pr_=pr_: e.tensor_copy(
                        PLR[:, 0:w], pr_[:, 0:w]), reads=[bpr_], writes=[bPLR])
                return fin
            return None

        def moe_gu(e_, apply_cb=True):
            for fp_ in range(0, MFT, 2):
                gv = fetch(dgp[e_, fp_:fp_ + 2], ("gp", e_, fp_))
                uv = fetch(dup[e_, fp_:fp_ + 2], ("up", e_, fp_))
                for oi in range(2):
                    fo = fp_ + oi
                    pg, bpg = nsp()
                    pu, bpu = nsp()
                    wga, bwga = gv[oi]
                    wua, bwua = uv[oi]
                    S.g("tensor", acc_group(
                        pg, [(0, C)],
                        lambda kt, wga=wga: wga[:, kt * 128:(kt + 1) * 128],
                        lambda kt, off_, wc_: tile(XNB[:], kt, C, ono), FT),
                        reads=bX_all + [bwga], writes=[bpg])
                    S.g("tensor", acc_group(
                        pu, [(0, C)],
                        lambda kt, wua=wua: wua[:, kt * 128:(kt + 1) * 128],
                        lambda kt, off_, wc_: tile(XNB[:], kt, C, ono), FT),
                        reads=bX_all + [bwua], writes=[bpu])
                    S.g("scalar", lambda e, pg=pg: e.activation(
                        TM2[:, 0:C], pg[:, 0:C], AF.Silu),
                        reads=[bpg], writes=[bTM2])
                    if apply_cb:
                        S.g("vector", lambda e, pu=pu: e.tensor_tensor(
                            TM1[:], TM2[:, 0:C], pu[:, 0:C], ALU.mult),
                            reads=[bTM2, bpu], writes=[bTM1])
                        S.g("vector", lambda e, fo=fo, e_=e_: e.tensor_tensor(
                            ACT8[:, fo * C:(fo + 1) * C], TM1[:],
                            CBp[:, e_ * C:(e_ + 1) * C], ALU.mult),
                            reads=[bTM1, bCB], writes=[bACT], extra=[bACT.w])
                    else:
                        S.g("vector", lambda e, fo=fo, pu=pu: e.tensor_tensor(
                            ACT8[:, fo * C:(fo + 1) * C], TM2[:, 0:C],
                            pu[:, 0:C], ALU.mult),
                            reads=[bTM2, bpu], writes=[bACT], extra=[bACT.w])

        def moe_down(e_, pre=None, defer=False):
            last_e = (e_ == E - 1)
            shared = {}
            units = []

            def unit(do):
                dp_ = (do // 4) * 4
                j = do % 4
                if j == 0:
                    shared[dp_] = fetch(ddp[e_, dp_:dp_ + 4], ("dp", e_, dp_))
                dv = shared[dp_]
                if pre is not None:
                    pre(do)
                pk, bpk = nsp()
                wda, bwda = dv[j]
                S.g("tensor", acc_group(
                    pk, [(0, C)],
                    lambda fl, wda=wda: wda[:, fl * 128:(fl + 1) * 128],
                    lambda fl, off_, wc_: ACT8[:, fl * C:(fl + 1) * C], MFT),
                    reads=[bACT, bwda], writes=[bpk])
                if not last_e:
                    if e_ == 0:
                        S.g("vector", lambda e, do=do, pk=pk: e.tensor_tensor(
                            tile(Bf[:], do, C), pk[:, 0:C],
                            CBp[:, 0 * C:1 * C], ALU.mult),
                            reads=[bpk, bCB], writes=[bBt[do]])
                    else:
                        S.g("vector", lambda e, do=do, pk=pk: e.tensor_tensor(
                            tile(Bf[:], do, C), tile(Bf[:], do, C),
                            pk[:, 0:C], ALU.add),
                            reads=[bpk], writes=[bBt[do]])
                else:
                    if do % 2 == 0:
                        Ta, bTa, Tb, bTb = T0, bT0, T1, bT1
                        stsem = "st0" if ci == 0 else "gst0"
                    else:
                        Ta, bTa, Tb, bTb = PLR, bPLR, PLM, bPLM
                        stsem = "st1" if ci == 0 else "gst1"
                    own0 = 2 + ci * C

                    def xr_load(k):
                        if k < FT:
                            pi = k % 3
                            S.g("scalar", lambda e, k=k, pi=pi: e.dma_start(
                                out=PO4[pi][:],
                                in_=dx[k * 128:(k + 1) * 128, own0:own0 + C]),
                                writes=[bPO[pi]], dma=True,
                                sem=f"xr{pi}" if pi < 2 else "xr2")
                    if do == 0:
                        for k in range(3):
                            xr_load(k)
                    pi = do % 3
                    S.g("vector", lambda e, do=do, pk=pk, Ta=Ta:
                        e.tensor_tensor(
                            Ta[:, 0:C], tile(Bf[:], do, C), pk[:, 0:C],
                            ALU.add),
                        reads=[bBt[do], bpk], writes=[bTa])
                    S.g("scalar", lambda e, Ta=Ta, Tb=Tb: e.activation(
                        Tb[:, 0:C], Ta[:, 0:C], AF.Silu),
                        reads=[bTa], writes=[bTb])
                    S.g("vector", lambda e, do=do, Ta=Ta, Tb=Tb:
                        e.tensor_tensor(
                            Ta[:, 0:C], Tb[:, 0:C], tile(A[:], do, C, ono),
                            ALU.mult),
                        reads=[bTb, bAt[do]], writes=[bTa], extra=[bTa.w])
                    S.g("vector", lambda e, pi=pi, Ta=Ta: e.tensor_tensor(
                        Ta[:, 0:C], Ta[:, 0:C], PO4[pi][:], ALU.add),
                        reads=[bPO[pi]], writes=[bTa], extra=[bTa.w])
                    xr_load(do + 3)
                    steng = "sync" if ci == 0 else "gpsimd"
                    S.g(steng, lambda e, do=do, Ta=Ta: e.dma_start(
                        out=dout[do * 128:(do + 1) * 128,
                                 ci * C:(ci + 1) * C],
                        in_=Ta[:, 0:C]),
                        reads=[bTa], dma=True, sem=stsem)

            if defer:
                return [(lambda do=do: unit(do)) for do in range(FT)]
            for do in range(FT):
                unit(do)
            return None

        PHASE[0] = "LN0"

        def rep_row(row_ap, dst_ap, dst_b, w):
            pr_, bpr_ = nsp()
            MM_LOG.append((PHASE[0], 1))
            S.g("tensor", lambda e, pr_=pr_, row_ap=row_ap, w=w: e.matmul(
                pr_[:, 0:w], ONE1[:], row_ap, start=True, stop=True),
                reads=[bROWS, bconst], writes=[bpr_])
            S.g("vector", lambda e, pr_=pr_, dst_ap=dst_ap, w=w: e.tensor_copy(
                dst_ap[:, 0:w], pr_[:, 0:w]), reads=[bpr_], writes=[dst_b])

        if ci == 0:
            ln_stats(lambda ft: tile(A[:], ft, cw), cw, 0, P[2], bP[2],
                     P[3], bP[3], bA_all, False)
            fin_ln1 = rows_and_planes(P[2], bP[2], P[3], bP[3], cw,
                                      compose_ln1=True)
        else:
            rep_row(ROWS[0:1, C:C + cw], PLR, bPLR, cw)
            rep_row(ROWS[0:1, 0:cw], PLM, bPLM, cw)

            def fin_ln1():
                rep_row(ROWS[0:1, 2 * C:2 * C + cw], PLR, bPLR, cw)
        xeng = "vector"
        for ft in range(FT):
            S.g(xeng, lambda e, ft=ft: e.tensor_tensor(
                tile(A[:], ft, cw), tile(A[:], ft, cw), PLR[:, 0:cw], ALU.mult),
                reads=[bPLR], writes=[bAt[ft]])
            S.g(xeng, lambda e, ft=ft: e.tensor_tensor(
                tile(A[:], ft, cw), tile(A[:], ft, cw), PLM[:, 0:cw],
                ALU.subtract), reads=[bPLM], writes=[bAt[ft]])
            S.g(xeng, lambda e, ft=ft: e.tensor_copy(
                tile(XNB[:], ft, cw), tile(A[:], ft, cw)),
                reads=[bAt[ft]], writes=[bXt[ft]])
        fin_ln1()

        def h1_sweep_ft(ft, mix_specs):
            hw = cw if ci == 0 else 513
            if ci > 0:
                S.g("vector", lambda e, ft=ft: e.tensor_copy(
                    T0[:, 0:1], H1C[:, ((ci - 1) % 2) * FT + ft:
                                    ((ci - 1) % 2) * FT + ft + 1]),
                    reads=[bH1C], writes=[bT0])
            ho = 0 if ci == 0 else 1
            S.g("vector", lambda e, ft=ft, ho=ho: e.tensor_tensor(
                T0[:, ho:ho + cw], tile(A[:], ft, cw), PLR[:, 0:cw],
                ALU.mult), reads=[bAt[ft], bPLR], writes=[bT0],
                extra=[bT0.w])
            S.g("vector", lambda e, ft=ft, ho=ho: e.tensor_scalar(
                T0[:, ho:ho + cw], T0[:, ho:ho + cw], pp(P_G1, ft),
                pp(P_B1, ft), ALU.mult, ALU.add),
                reads=[bconst], writes=[bT0], extra=[bT0.w])
            if ci == 0:
                S.g("vector", lambda e: e.tensor_scalar_mul(
                    T0[:, 0:2], T0[:, 0:2], MSK[:]),
                    reads=[bconst], writes=[bT0], extra=[bT0.w])
            S.g("vector", lambda e, ft=ft, hw=hw: e.tensor_copy(
                H1C[:, (ci % 2) * FT + ft:(ci % 2) * FT + ft + 1],
                T0[:, hw - 1:hw]),
                reads=[bT0], writes=[bH1C])
            S.g("vector", lambda e, hw=hw: e.tensor_tensor(
                T1[:, 0:hw - 1], T0[:, 1:hw], T0[:, 0:hw - 1],
                ALU.subtract), reads=[bT0], writes=[bT1])
            for (dstap, dstb, pb) in mix_specs:
                S.g("vector", lambda e, ft=ft, dstap=dstap, pb=pb, hw=hw:
                    e.scalar_tensor_tensor(
                        tile(dstap, ft, hw - 1), T1[:, 0:hw - 1],
                        pp(pb, ft), T0[:, 0:hw - 1], ALU.mult, ALU.add),
                    reads=[bT1, bT0, bconst], writes=[dstb],
                    extra=[dstb.w])

        def h1_sweep(mix_specs):
            for ft in range(FT):
                h1_sweep_ft(ft, mix_specs)

        h1_sweep([(M1[:], bM1, P_TMR), (M2[:], bM2, P_TMV)])

        PHASE[0] = "scores"
        p3, bp3 = P[3], bP[3]
        S.g("tensor", acc_group(
            p3[0:8, :], [(0, C)], lambda kt: RTRB[:, kt * 8:(kt + 1) * 8],
            lambda kt, off_, wc_: tile(XNB[:], kt, C, ono), FT),
            reads=bX_all + [bconst], writes=[bp3])
        S.g("vector", lambda e: e.tensor_copy(SCR[:], p3[0:8, 0:C]),
            reads=[bp3], writes=[bSCR])
        PHASE[0] = "e0gu"
        moe_gu(0, apply_cb=False)
        for blk in range(4):
            pq, bpq = nsp()
            MM_LOG.append((PHASE[0], 1))
            S.g("tensor", lambda e, blk=blk, pq=pq: e.transpose(
                pq[:, 0:8], SCR[:, blk * 128:(blk + 1) * 128], ID8[:]),
                reads=[bSCR, bconst], writes=[bpq])
            st = TT8[:, 0:8]
            m1 = TT8[:, 8:9]; m2 = TT8[:, 9:10]; nm1 = TT8[:, 10:11]
            wv_ = TT8[:, 16:24]; mk2 = TT8[:, 24:32]; s2 = TT8[:, 32:40]
            S.g("vector", lambda e, pq=pq: e.tensor_copy(st, pq[:, 0:8]),
                reads=[bpq], writes=[bTT8], extra=[bTT8.w])
            S.g("vector", lambda e: e.tensor_reduce(m1, st, AX.X, ALU.max),
                reads=[bTT8], writes=[bTT8], extra=[bTT8.w])
            S.g("vector", lambda e: e.tensor_scalar(s2, st, m1, None, ALU.is_ge),
                reads=[bTT8], writes=[bTT8], extra=[bTT8.w])
            S.g("vector", lambda e: e.scalar_tensor_tensor(
                s2, s2, -3e38, st, ALU.mult, ALU.add),
                reads=[bTT8], writes=[bTT8], extra=[bTT8.w])
            S.g("vector", lambda e: e.tensor_reduce(m2, s2, AX.X, ALU.max),
                reads=[bTT8], writes=[bTT8], extra=[bTT8.w])
            S.g("vector", lambda e: e.tensor_scalar_mul(nm1, m1, -1.0),
                reads=[bTT8], writes=[bTT8], extra=[bTT8.w])
            S.g("scalar", lambda e: e.activation(wv_, st, AF.Exp, bias=nm1),
                reads=[bTT8], writes=[bTT8], extra=[bTT8.w])
            S.g("vector", lambda e: e.tensor_scalar(mk2, st, m2, None, ALU.is_ge),
                reads=[bTT8], writes=[bTT8], extra=[bTT8.w])
            S.g("vector", lambda e: e.tensor_tensor(wv_, wv_, mk2, ALU.mult),
                reads=[bTT8], writes=[bTT8], extra=[bTT8.w])
            S.g("vector", lambda e: e.tensor_tensor(m2, m2, nm1, ALU.add),
                reads=[bTT8], writes=[bTT8], extra=[bTT8.w])
            S.g("scalar", lambda e: e.activation(m2, m2, AF.Exp),
                reads=[bTT8], writes=[bTT8], extra=[bTT8.w])
            S.g("vector", lambda e: e.tensor_scalar_add(m2, m2, 1.0),
                reads=[bTT8], writes=[bTT8], extra=[bTT8.w])
            S.g("vector", lambda e: e.reciprocal(m2, m2),
                reads=[bTT8], writes=[bTT8], extra=[bTT8.w])
            S.g("vector", lambda e, blk=blk: e.tensor_scalar_mul(
                CTM[:, blk * 8:(blk + 1) * 8], wv_, m2),
                reads=[bTT8], writes=[bCTM], extra=[bCTM.w])
        for blk in range(4):
            pq, bpq = nsp()
            MM_LOG.append((PHASE[0], 1))
            S.g("tensor", lambda e, blk=blk, pq=pq: e.transpose(
                pq[0:8, 0:128], CTM[:, blk * 8:(blk + 1) * 8], ID128[:]),
                reads=[bCTM, bconst], writes=[bpq])
            S.g("vector", lambda e, blk=blk, pq=pq: e.tensor_copy(
                CFM[:, blk * 128:(blk + 1) * 128], pq[0:8, 0:128]),
                reads=[bpq], writes=[bCFM], extra=[bCFM.w])
        for e_ in range(E):
            pq, bpq = nsp()
            MM_LOG.append((PHASE[0], 1))
            S.g("tensor", lambda e, e_=e_, pq=pq: e.matmul(
                pq[:, 0:C], SEL[:, e_ * 128:(e_ + 1) * 128], CFM[:],
                start=True, stop=True),
                reads=[bCFM, bconst], writes=[bpq])
            S.g("vector", lambda e, e_=e_, pq=pq: e.tensor_copy(
                CBp[:, e_ * C:(e_ + 1) * C], pq[:, 0:C]),
                reads=[bpq], writes=[bCB], extra=[bCB.w])



        PHASE[0] = "vpass"
        for pr in range(0, FT, 2):
            vv_ = fetch(dwv[pr:pr + 2], ("wv", pr))
            for oi in range(2):
                ot = pr + oi
                pv, bpv = nsp()
                wva, bwva = vv_[oi]
                S.g("tensor", acc_group(
                    pv, spans,
                    lambda kt, wva=wva: wva[:, kt * 128:(kt + 1) * 128],
                    lambda kt, off_, wc_: tile(M2[:], kt, wc_, off_), FT),
                    reads=[bM2, bwva], writes=[bpv])
                S.g("vector", lambda e, ot=ot, pv=pv: e.tensor_copy(
                    tile(M3[:], ot, mw), pv[:, 0:mw]),
                    reads=[bpv], writes=[bM3], extra=[bM3.w])

        moe_down(0)
        PHASE[0] = "e1gu"
        moe_gu(1)

        PHASE[0] = "rpass"
        for pr in range(0, FT, 2):
            rv_ = fetch(dwr[pr:pr + 2], ("wr", pr))
            for oi in range(2):
                ot = pr + oi
                pk, bpk = nsp()
                wra, bwra = rv_[oi]
                S.g("tensor", acc_group(
                    pk, spans,
                    lambda kt, wra=wra: wra[:, kt * 128:(kt + 1) * 128],
                    lambda kt, off_, wc_: tile(M1[:], kt, wc_, off_), FT),
                    reads=[bM1, bwra], writes=[bpk])
                S.g("scalar", lambda e, pk=pk: e.activation(
                    TM2[:, 0:mw], pk[:, 0:mw], AF.Sigmoid),
                    reads=[bpk], writes=[bTM2])
                S.g("vector", lambda e, ot=ot: e.tensor_tensor(
                    tile(M3[:], ot, mw), TM2[:, 0:mw], tile(M3[:], ot, mw),
                    ALU.mult), reads=[bTM2], writes=[bM3], extra=[bM3.w])

        PHASE[0] = "e1dn_e2gu"
        moe_down(1)
        moe_gu(2)

        PHASE[0] = "wo"
        for pr in range(0, FT, 2):
            ov_ = fetch(dwo[pr:pr + 2], ("wo", pr))
            for oi in range(2):
                ot = pr + oi
                pk, bpk = nsp()
                woa, bwoa = ov_[oi]
                S.g("tensor", acc_group(
                    pk, spans,
                    lambda kt, woa=woa: woa[:, kt * 128:(kt + 1) * 128],
                    lambda kt, off_, wc_: tile(M3[:], kt, wc_, off_), FT),
                    reads=[bM3, bwoa], writes=[bpk])
                S.g("vector", lambda e, ot=ot, pk=pk: e.tensor_tensor(
                    tile(A[:], ot, mw, ox1), tile(A[:], ot, mw, ox1),
                    pk[:, 0:mw], ALU.add),
                    reads=[bpk], writes=[bAt[ot]])

        PHASE[0] = "LN2"
        ln_stats(lambda ft: tile(A[:], ft, mw, ox1), mw, ox1, P[2], bP[2],
                 P[3], bP[3], bA_all, False)
        rows_and_planes(P[2], bP[2], P[3], bP[3], mw)

        def h2_ft(ft):
            if ci > 0:
                S.g("vector", lambda e, ft=ft: e.tensor_copy(
                    T0[:, 0:1], H2C[:, ft:ft + 1]),
                    reads=[bH2C], writes=[bT0])
            ho = 0 if ci == 0 else 1
            S.g("vector", lambda e, ft=ft, ho=ho: e.tensor_tensor(
                T0[:, ho:ho + mw], tile(A[:], ft, mw, ox1), PLR[:, 0:mw],
                ALU.mult), reads=[bAt[ft], bPLR], writes=[bT0], extra=[bT0.w])
            S.g("vector", lambda e, ho=ho: e.tensor_tensor(
                T0[:, ho:ho + mw], T0[:, ho:ho + mw], PLM[:, 0:mw],
                ALU.subtract), reads=[bPLM], writes=[bT0], extra=[bT0.w])
            S.g("vector", lambda e, ft=ft, ho=ho: e.tensor_scalar(
                T0[:, ho:ho + mw], T0[:, ho:ho + mw], pp(P_G2, ft),
                pp(P_B2, ft), ALU.mult, ALU.add),
                reads=[bconst], writes=[bT0], extra=[bT0.w])
            if ci == 0:
                S.g("vector", lambda e: e.tensor_scalar_mul(
                    T0[:, 0:1], T0[:, 0:1], MSK[:]),
                    reads=[bconst], writes=[bT0], extra=[bT0.w])
            S.g("vector", lambda e, ft=ft: e.tensor_copy(
                H2C[:, ft:ft + 1], T0[:, 512:513]),
                reads=[bT0], writes=[bH2C])
            S.g("vector", lambda e: e.tensor_tensor(
                T1[:, 0:C], T0[:, 1:513], T0[:, 0:C], ALU.subtract),
                reads=[bT0], writes=[bT1])
            S.g("vector", lambda e, ft=ft: e.scalar_tensor_tensor(
                tile(M1[:], ft, C), T1[:, 0:C], pp(P_CMK, ft),
                T0[:, 0:C], ALU.mult, ALU.add),
                reads=[bT1, bT0, bconst], writes=[bM1], extra=[bM1.w])
            S.g("vector", lambda e, ft=ft: e.scalar_tensor_tensor(
                tile(M3[:], ft, C), T1[:, 0:C], pp(P_CMR, ft),
                T0[:, 0:C], ALU.mult, ALU.add),
                reads=[bT1, bT0, bconst], writes=[bM3], extra=[bM3.w])

        PHASE[0] = "e2dn_h2"
        moe_down(2, pre=h2_ft)
        PHASE[0] = "e3gu"
        moe_gu(3)

        PHASE[0] = "wcr"
        for pr in range(0, FT, 2):
            cv_ = fetch(dwcr[pr:pr + 2], ("wcr", pr))
            for oi in range(2):
                ot = pr + oi
                pk, bpk = nsp()
                wca, bwca = cv_[oi]
                S.g("tensor", acc_group(
                    pk, [(0, C)],
                    lambda kt, wca=wca: wca[:, kt * 128:(kt + 1) * 128],
                    lambda kt, off_, wc_: tile(M3[:], kt, C), FT),
                    reads=[bM3, bwca], writes=[bpk])
                S.g("scalar", lambda e, pk=pk: e.activation(
                    TM2[:, 0:C], pk[:, 0:C], AF.Sigmoid),
                    reads=[bpk], writes=[bTM2])
                S.g("vector", lambda e, ot=ot: e.tensor_copy(
                    tile(M2[:], ot, C), TM2[:, 0:C]),
                    reads=[bTM2], writes=[bM2], extra=[bM2.w])

        PHASE[0] = "e3dn"
        moe_down(3)
        PHASE[0] = "e4gu"
        moe_gu(4)

        es_units = None
        if ci < NCH - 1:
            xoff1 = 2 + (ci + 1) * C

            def es_unit(u):
                for ft in (2 * u, 2 * u + 1):
                    sl = ft % 2
                    ra = TSQ[:, sl * STRIDE:sl * STRIDE + C]
                    rsq = TSQ[:, (2 + sl) * STRIDE:(2 + sl) * STRIDE + C]
                    bra, brsq = bTSQ[sl], bTSQ[2 + sl]
                    S.g("gpsimd", lambda e, ft=ft, ra=ra: e.dma_start(
                        out=ra, in_=dx[ft * 128:(ft + 1) * 128,
                                       xoff1:xoff1 + C]),
                        writes=[bra], dma=True, sem=f"xa{sl}")
                    S.g("scalar", lambda e, ra=ra, rsq=rsq: e.activation(
                        rsq, ra, AF.Square), reads=[bra], writes=[brsq])
                    MM_LOG.append((PHASE[0], 1))
                    S.g("tensor", lambda e, ft=ft, ra=ra: e.matmul(
                        P[2][0:1, 0:C], ONEB[:], ra, start=(ft == 0),
                        stop=(ft == FT - 1)),
                        reads=[bra, bconst], writes=[bP[2]])
                    MM_LOG.append((PHASE[0], 1))
                    S.g("tensor", lambda e, ft=ft, rsq=rsq: e.matmul(
                        P[3][0:1, 0:C], ONEB[:], rsq, start=(ft == 0),
                        stop=(ft == FT - 1)),
                        reads=[brsq, bconst], writes=[bP[3]])
            es_units = [lambda u=u: es_unit(u) for u in range(8)]

        PHASE[0] = "wgrp"
        for grp in range(4):
            PHASE[0] = f"wkey{grp}"
            for fp_ in range(0, 16, 2):
                fo0 = grp * 16 + fp_
                kyv = fetch(dwkey[fo0:fo0 + 2], ("wkey", fo0))
                for oi in range(2):
                    fl = fp_ + oi
                    pk, bpk = nsp()
                    wya, bwya = kyv[oi]
                    S.g("tensor", acc_group(
                        pk, [(0, C)],
                        lambda kt, wya=wya: wya[:, kt * 128:(kt + 1) * 128],
                        lambda kt, off_, wc_: tile(M1[:], kt, C), FT),
                        reads=[bM1, bwya], writes=[bpk])
                    S.g("scalar", lambda e, fl=fl, pk=pk: e.activation(
                        tile(M3[:], fl, C), pk[:, 0:C], AF.Silu),
                        reads=[bpk], writes=[bM3], extra=[bM3.w])
            PHASE[0] = f"wval{grp}"
            if grp < 3:
                units = moe_down(4 + grp, defer=True)
                if grp == 2 and es_units is not None:
                    resv[0] = True
            else:
                PHASE[0] = "finale"
                units = moe_down(7, defer=True)
                PHASE[0] = f"wval{grp}"
            for op_ in range(0, 16, 2):
                if grp == 2 and es_units is not None:
                    es_units[op_ // 2]()
                wvv = fetch(dwval[op_:op_ + 2, :, grp * 2048:(grp + 1) * 2048], ("wval", grp, op_))
                for oi in range(2):
                    ov = op_ + oi
                    pk, bpk = nsp()
                    wvla, bwvla = wvv[oi]
                    S.g("tensor", acc_group(
                        pk, [(0, C)],
                        lambda fl, wvla=wvla: wvla[:, fl * 128:(fl + 1) * 128],
                        lambda fl, off_, wc_: tile(M3[:], fl, C), FT),
                        reads=[bM3, bwvla], writes=[bpk])
                    S.g("vector", lambda e, ov=ov, pk=pk: e.tensor_tensor(
                        TM2[:, 0:C], pk[:, 0:C], tile(M2[:], ov, C), ALU.mult),
                        reads=[bpk, bM2], writes=[bTM2])
                    S.g("vector", lambda e, ov=ov: e.tensor_tensor(
                        tile(A[:], ov, C, ono), tile(A[:], ov, C, ono),
                        TM2[:, 0:C], ALU.add),
                        reads=[bTM2], writes=[bAt[ov]])
                    units[ov]()
            if grp == 2 and es_units is not None:
                R0e = ROWS[0:1, 0:C]
                R1e = ROWS[0:1, C:2 * C]
                R2e = ROWS[0:1, 2 * C:3 * C]

                def rv(fn, rd=False):
                    S.g("vector", fn,
                        reads=[bP[2], bP[3]] if rd else [],
                        writes=[bROWS], extra=[bROWS.w])

                def rs(fn):
                    S.g("scalar", fn, reads=[], writes=[bROWS],
                        extra=[bROWS.w])
                rv(lambda e: e.tensor_scalar_mul(R0e, P[2][0:1, 0:C], 1.0 / H),
                   rd=True)
                rv(lambda e: e.tensor_scalar_mul(R1e, P[3][0:1, 0:C], 1.0 / H),
                   rd=True)
                rv(lambda e: e.tensor_tensor(R2e, R0e, R0e, ALU.mult))
                rv(lambda e: e.tensor_tensor(R1e, R1e, R2e, ALU.subtract))
                rv(lambda e: e.tensor_copy(R2e, R1e))
                rv(lambda e: e.tensor_scalar_add(R1e, R1e, EPS))
                rs(lambda e: e.activation(R1e, R1e, AF.Ln))
                rs(lambda e: e.activation(R1e, R1e, AF.Exp, scale=-0.5))
                rv(lambda e: e.tensor_tensor(R0e, R0e, R1e, ALU.mult))
                rv(lambda e: e.tensor_tensor(R2e, R2e, R1e, ALU.mult))
                rv(lambda e: e.tensor_tensor(R2e, R2e, R1e, ALU.mult))
                rv(lambda e: e.tensor_scalar_add(R2e, R2e, EPS))
                rs(lambda e: e.activation(R2e, R2e, AF.Ln))
                rs(lambda e: e.activation(R2e, R2e, AF.Exp, scale=-0.5))
                resv[0] = False
            if grp < 3:
                PHASE[0] = f"e{5+grp}gu"
                moe_gu(5 + grp)


    for ci in range(NCH):
        PHASE[0] = f"c{ci}"
        emit_chunk(ci)
    assert len(blk_map) == NBLK, f"block count {len(blk_map)} != NBLK {NBLK}"

    nst = {k: S.count[k] for k in ("st0", "st1", "gst0", "gst1")}

    sem_es = ExitStack()
    sems = {}
    for name in ENGINES + DMA_SEMS:
        sems[name] = sem_es.enter_context(nc.semaphore(f"s_{name}"))
    with nc.Block() as block:

        @block.tensor
        def _(e):
            S.emit_engine(e, "tensor", sems)

        @block.vector
        def _(e):
            S.emit_engine(e, "vector", sems)

        @block.scalar
        def _(e):
            S.emit_engine(e, "scalar", sems)

        @block.sync
        def _(e):
            S.emit_engine(e, "sync", sems)
            for k_ in ("st0", "st1", "gst0", "gst1"):
                e.wait_ge(sems[k_], nst[k_])

        @block.gpsimd
        def _(e):
            S.emit_engine(e, "gpsimd", sems)

    sem_es.close()
    es.close()
    return nc


_NC_CACHE = [None]


def kernel(**inputs):
    per_core = _prep_inputs(inputs)
    if _NC_CACHE[0] is None:
        _NC_CACHE[0] = build_program()
    nc = _NC_CACHE[0]
    res = run_bass_kernel_spmd(nc, per_core, core_ids=list(range(CORES)))
    out = np.empty((B, T, H), dtype=np.float32)
    for c in range(CORES):
        b, half = c // 2, c % 2
        out[b, half * TOK:(half + 1) * TOK, :] = res.results[c]["out"].T
    return out
